# version 13 (final)
# baseline (speedup 1.0000x reference)
import numpy as np
from contextlib import ExitStack

import concourse.bass as bass
import concourse.mybir as mybir
from concourse.bass_utils import run_bass_kernel_spmd

F32 = mybir.dt.float32
BF16 = mybir.dt.bfloat16
AF = mybir.ActivationFunctionType
ALU = mybir.AluOpType
AX = mybir.AxisListType

TRUNC = None
DBG_TILES = ()
D, H, DHEAD, WINDOW, PADW = 1024, 16, 64, 64, 32
L, T, DHID = 2048, 512, 4096
NSEQ, NTILE, NCORE = 2, 8, 8

C_NEGINVD, C_INVD, C_EPS, C_NEGHALF, C_SMSCALE, C_NEGSM = 0, 1, 2, 3, 4, 5

WMETA = {
    "wq": (D, D), "wk": (D, D), "wv": (D, D), "wo": (D, D),
    "w1": (D, DHID), "w2": (D, DHID),
    "h1": (DHID, DHID), "h2": (DHID, DHID),
    "w3": (DHID, D),
}


def _ogroups(noc, g):
    out, i = [], 0
    while i < noc:
        n = min(g, noc - i)
        out.append(list(range(i, i + n)))
        i += n
    return out


def granule_plan():
    plan = []
    for w in ("wq", "wk"):
        for grp in _ogroups(8, 2):
            plan.append((w, 0, 8, grp[0] * 128, len(grp) * 128))
    for w in ("wv",):
        for (o0, oc) in ((0, 384), (384, 384), (768, 256)):
            plan.append((w, 0, 8, o0, oc))
    for w in ("wo",):
        for (o0, oc) in ((0, 384), (384, 384), (768, 256)):
            plan.append((w, 0, 8, o0, oc))
    for w in ("w1", "w2"):
        for grp in _ogroups(32, 2):
            plan.append((w, 0, 8, grp[0] * 128, len(grp) * 128))
    for w in ("h1", "h2"):
        for grp in _ogroups(32, 2):
            for kq in range(4):
                plan.append((w, kq * 8, 8, grp[0] * 128, len(grp) * 128))
    for oq in range(4):
        for kh in range(2):
            plan.append(("w3", kh * 16, 16, oq * 256, 256))
    return plan


GRANS = granule_plan()
NGRAN = len(GRANS)
GIDX = {(w, kc0, o0): i for i, (w, kc0, nkc, o0, nc_) in enumerate(GRANS)}
BG_WNAMES = ("wq", "wk", "wv", "wo")
SLOT_BASES = [0, 4096, 8192, 12288, 15360]
FG_SLOTS = (0, 1, 2)
BG_SLOTS = (3, 4)
WSL_COLS = 18432

FSC_F = 0
FSC_A = 1024
FSC_W3 = 1408
FSC_WO = 1664
FSC_COLS = 2048


class Sem:
    def __init__(self, handle, owner):
        self.h = handle
        self.owner = owner
        self.n = 0


class Sched:
    ENGINES = ("tensor", "vector", "scalar", "gpsimd", "sync")

    def __init__(self):
        self.ops = {e: [] for e in self.ENGINES}
        self.waited = {e: {} for e in self.ENGINES}
        self.dep = {}
        self.seq = 0

    def do(self, eng, fn, waits=(), inc=None, inck=1):
        import sys as _sys
        self.last_line = _sys._getframe(1).f_lineno
        real = []
        w = self.waited[eng]
        for sv in waits:
            if sv is None:
                continue
            s, v = sv
            if v <= 0 or s.owner == eng:
                continue
            if w.get(id(s), -1) >= v:
                continue
            real.append((s, v))
            w[id(s)] = v
        best = {}
        for (s, v) in real:
            if v > best.get(id(s), (None, -1))[1]:
                best[id(s)] = (s, v)
        real = list(best.values())
        res = None
        if inc is not None:
            inc.n += inck
            res = (inc, inc.n)
        self.ops[eng].append((real, fn, inc, inck, self.seq))
        self.trace = getattr(self, "trace", [])
        self.trace.append((self.seq, eng, self.last_line,
                           [(id(s) % 1000, v) for (s, v) in real],
                           (id(inc) % 1000 if inc else None, inc.n if inc else None)))
        self.seq += 1
        return res

    def need(self, *keys):
        out = []
        for k in keys:
            out.extend(self.dep.get(k, ()))
        return out

    def setdep(self, key, sv):
        if sv is not None:
            self.dep[key] = [sv]

    def adddep(self, key, sv):
        if sv is not None:
            self.dep.setdefault(key, []).append(sv)

    def replay(self, eng_name, engine):
        for (waits, fn, inc, inck, seq) in self.ops[eng_name]:
            if TRUNC is not None and seq >= TRUNC:
                continue
            for (s, v) in waits:
                engine.wait_ge(s.h, v)
            ins = fn(engine)
            if inc is not None:
                ins.then_inc(inc.h, inck)


def build_program():
    nc = bass.Bass("TRN2", target_bir_lowering=False)

    x_in = nc.dram_tensor("x", [NSEQ * L, D], F32, kind="ExternalInput").ap()
    wt = {w: nc.dram_tensor(w + "T", [k, o], F32, kind="ExternalInput").ap()
          for w, (k, o) in WMETA.items()}
    sc_in = {n: nc.dram_tensor(n, [128, 32], F32, kind="ExternalInput").ap()
             for n in ("s1", "s2", "sh1", "sh2")}
    bo_in = nc.dram_tensor("bo_bc", [128, D], F32, kind="ExternalInput").ap()
    w3s_in = nc.dram_tensor("w3s_bc", [128, D], F32, kind="ExternalInput").ap()
    mask_in = nc.dram_tensor("mask", [128, 192], F32, kind="ExternalInput").ap()
    id_in = nc.dram_tensor("ident", [128, 128], F32, kind="ExternalInput").ap()
    cv_in = nc.dram_tensor("cvec", [128, 8], F32, kind="ExternalInput").ap()
    y_out = nc.dram_tensor("y", [NSEQ * L, D], F32, kind="ExternalOutput").ap()
    dbg_out = {}

    scr = [nc.dram_tensor(f"scr{i}", [128, nkc * ncols], BF16, kind="Internal").ap()
           for i, (w, kc0, nkc, o0, ncols) in enumerate(GRANS)]

    ctx = ExitStack()
    E = ctx.enter_context
    with ctx:
        sb = {}
        sb["lnq0"] = E(nc.sbuf_tensor("lnq0", [128, D], BF16))
        sb["lnq1"] = E(nc.sbuf_tensor("lnq1", [128, D], BF16))
        sb["sqd"] = sb["lnq1"]
        sb["ln1"] = E(nc.sbuf_tensor("ln1", [128, 8 * 576], BF16))
        sb["ln2"] = E(nc.sbuf_tensor("ln2", [128, 8 * 512], BF16))
        sb["qat"] = E(nc.sbuf_tensor("qat", [128, 8 * 512], BF16))
        sb["kfm"] = E(nc.sbuf_tensor("kfm", [128, 8 * 576], BF16))
        sb["vtm"] = E(nc.sbuf_tensor("vtm", [128, 5 * 1024], BF16))
        sb["g"] = E(nc.sbuf_tensor("g", [128, 32 * 512], BF16))
        sb["h"] = E(nc.sbuf_tensor("h", [128, 32 * 512], BF16))
        sb["apb"] = E(nc.sbuf_tensor("apb", [128, 384], BF16))
        sb["pnb"] = E(nc.sbuf_tensor("pnb", [128, 768], BF16))
        sb["pts"] = E(nc.sbuf_tensor("pts", [128, 1024], BF16))
        sb["idb"] = E(nc.sbuf_tensor("idb", [128, 128], BF16))
        sb["wsl"] = E(nc.sbuf_tensor("wsl", [128, WSL_COLS], BF16))
        sb["R0"] = E(nc.sbuf_tensor("R0", [128, 4 * 1024], F32))
        sb["R1"] = E(nc.sbuf_tensor("R1", [128, 4 * 1024], F32))
        sb["xh"] = E(nc.sbuf_tensor("xh", [64, 1024], F32))
        sb["fsc"] = E(nc.sbuf_tensor("fsc", [128, FSC_COLS], F32))
        sb["bo"] = E(nc.sbuf_tensor("bo", [128, D], F32))
        sb["w3s"] = E(nc.sbuf_tensor("w3s", [128, D], F32))
        sb["msk"] = E(nc.sbuf_tensor("msk", [128, 192], F32))
        sb["cv"] = E(nc.sbuf_tensor("cv", [128, 8], F32))
        for n in ("s1", "s2", "sh1", "sh2"):
            sb[n] = E(nc.sbuf_tensor("sb_" + n, [128, 32], F32))
        sb["st"] = E(nc.sbuf_tensor("st", [128, 32], F32))
        sb["sx"] = E(nc.sbuf_tensor("sx", [128, 8], F32))

        pbank = [E(nc.psum_tensor(f"pb{i}", [128, 512], F32)) for i in range(7)]
        pbf = E(nc.psum_tensor("pbf0", [128, 1024], BF16))

        s_pe = Sem(E(nc.semaphore("s_pe")), "tensor")
        s_dv = Sem(E(nc.semaphore("s_dv")), "vector")
        s_ac = Sem(E(nc.semaphore("s_ac")), "scalar")
        s_pl = Sem(E(nc.semaphore("s_pl")), "gpsimd")
        s_wg = Sem(E(nc.semaphore("s_wg")), "qg")
        s_ws = Sem(E(nc.semaphore("s_ws")), "qs")
        s_cv = Sem(E(nc.semaphore("s_cv")), "qs")
        s_x = Sem(E(nc.semaphore("s_x")), "qa")
        s_o = Sem(E(nc.semaphore("s_o")), "qa")

        S = Sched()

        def Rbuf(tile):
            return sb["R0"] if tile % 2 == 0 else sb["R1"]

        def ldc(dst, src):
            return S.do("scalar", lambda e, d=dst, s=src: e.dma_start(d, s),
                        inc=s_x, inck=16)
        for n in ("s1", "s2", "sh1", "sh2"):
            ldc(sb[n][:], sc_in[n])
        ldc(sb["bo"][:], bo_in)
        ldc(sb["w3s"][:], w3s_in)
        ldc(sb["msk"][:], mask_in)
        ldc(sb["cv"][:], cv_in)
        idb_w = S.do("gpsimd", lambda e: e.dma_start(sb["idb"][:], id_in),
                     inc=s_wg, inck=16)
        consts_done = [(s_x, s_x.n), idb_w]

        slot_free = [[] for _ in range(len(SLOT_BASES))]
        slot_ctr = {"fg": 0, "bg": 0}
        gran_loaded = [None] * NGRAN
        scr_written = [None] * NGRAN

        def load_granule(tile, gi):
            w, kc0, nkc, o0, ncols = GRANS[gi]
            pool = "bg" if w in BG_WNAMES else "fg"
            slots = BG_SLOTS if pool == "bg" else FG_SLOTS
            slot = slots[slot_ctr[pool] % len(slots)]
            slot_ctr[pool] += 1
            base = SLOT_BASES[slot]
            dst = sb["wsl"][:, base: base + nkc * ncols]
            waits = list(slot_free[slot])
            slot_free[slot] = []
            if tile == 0:
                src = wt[w].rearrange("(kc p) o -> p kc o", p=128)[:, kc0:kc0 + nkc, o0:o0 + ncols]
                sv = S.do("gpsimd", lambda e, d=dst, s=src: e.dma_start(d, s),
                          waits=waits, inc=s_wg, inck=16)
                gran_loaded[gi] = sv
                wv = S.do("sync", lambda e, d=scr[gi], s=dst: e.dma_start(d, s),
                    waits=[sv], inc=s_cv, inck=16)
                scr_written[gi] = wv
                slot_free[slot].append(wv)
            else:
                q = "gpsimd" if pool == "bg" else "sync"
                qsem = s_wg if pool == "bg" else s_ws
                sv = S.do(q, lambda e, d=dst, s=scr[gi]: e.dma_start(d, s),
                          waits=waits + [scr_written[gi]], inc=qsem, inck=16)
                gran_loaded[gi] = sv
            return base, slot

        def gslice(gi, base, kc_local, o_local, width):
            w, kc0, nkc, o0, ncols = GRANS[gi]
            off = base + kc_local * ncols + o_local
            return sb["wsl"][:, off: off + width]

        def bkey(i):
            return f"pb{i}"

        act_state = ["?"]

        def act_set(name):
            want = {"exp": 6, "ln": 6, "silu": 18, "gelu": 10}[name]
            if act_state[0] == want:
                return
            act_state[0] = want
            S.do("scalar", lambda e: e.drain())
            if want == 6:
                def loadset6(e):
                    return e.add_instruction(mybir.InstLoadActFuncSet(
                        name=e.bass.get_next_instruction_name(),
                        act_func_set_id=6, ins=[], outs=[]))
                S.do("scalar", loadset6)
            else:
                f = {18: AF.Silu, 10: AF.Gelu}[want]
                S.do("scalar", lambda e, f=f: e.preload_act_table(f))
            S.do("scalar", lambda e: e.drain())

        SA, SB, SM, SQ, SV, SL = 0, 5, 10, 15, 20, 25

        def sched_ln(tile, which):
            par = tile % 2
            R = Rbuf(tile)
            if which == 1:
                dst, dstw, out_key = sb["ln1"], 576, "ln1w"
                chunks = [("halo", None)] + [("al", k) for k in range(4)]
            else:
                dst, dstw, out_key = sb["ln2"], 512, "ln2w"
                chunks = [("al", k) for k in range(4)]
            st = sb["st"]
            nch = len(chunks)

            def srcof(kind, k):
                if kind == "halo":
                    return 64, sb["xh"][0:64, :], S.need("xhw")
                if which == 1:
                    return 128, R[:, k * 1024: (k + 1) * 1024], S.need(f"Rw{par}_{k}")
                return 128, R[:, k * 1024: (k + 1) * 1024], S.need(f"Rmid{par}_{k}")

            a1 = None
            d1 = None
            for i, (kind, k) in enumerate(chunks):
                pn, xs, w_src = srcof(kind, k)
                d1 = S.do("vector", lambda e, xs=xs, pn=pn, i=i: e.reduce_sum(
                    st[0:pn, SA + i:SA + i + 1], xs, axis=AX.X),
                    waits=w_src, inc=s_dv)
                a1 = S.do("scalar", lambda e, xs=xs, pn=pn, i=i: e.activation(
                    sb["sqd"][0:pn, :], xs, AF.Square,
                    accum_out=st[0:pn, SB + i:SB + i + 1]),
                    waits=w_src + S.need("lnq1"), inc=s_ac)
            yield
            dA = S.do("vector", lambda e: e.drain(), inc=s_dv)
            S.do("vector", lambda e, nch=nch: e.tensor_scalar(
                st[:, SM:SM + nch], st[:, SA:SA + nch],
                sb["cv"][:, C_NEGINVD:C_NEGINVD + 1], None, ALU.mult),
                waits=consts_done)
            S.do("vector", lambda e, nch=nch: e.tensor_scalar(
                st[:, SQ:SQ + nch], st[:, SB:SB + nch],
                sb["cv"][:, C_INVD:C_INVD + 1], None, ALU.mult),
                waits=[a1])
            S.do("vector", lambda e: e.drain())
            S.do("vector", lambda e, nch=nch: e.tensor_tensor(
                st[:, SV:SV + nch], st[:, SM:SM + nch], st[:, SM:SM + nch],
                ALU.mult))
            S.do("vector", lambda e: e.drain())
            S.do("vector", lambda e, nch=nch: e.tensor_tensor(
                st[:, SV:SV + nch], st[:, SQ:SQ + nch], st[:, SV:SV + nch],
                ALU.subtract))
            d3 = S.do("vector", lambda e: e.drain(), inc=s_dv)
            act_set("ln")
            S.do("scalar", lambda e, nch=nch: e.activation(
                st[:, SL:SL + nch], st[:, SV:SV + nch], AF.Ln,
                bias=sb["cv"][:, C_EPS:C_EPS + 1]), waits=[d3])
            a2 = S.do("scalar", lambda e, nch=nch: e.activation(
                st[:, SV:SV + nch], st[:, SL:SL + nch], AF.Exp,
                scale=sb["cv"][:, C_NEGHALF:C_NEGHALF + 1]), inc=s_ac)
            yield
            if which == 1:
                for _ in range(2):
                    yield
            dlist = [None] * nch

            def emit_apply(i):
                kind, k = chunks[i]
                pn, xs, w_src = srcof(kind, k)
                lq = sb["lnq0"] if i % 2 == 0 else sb["lnq1"]
                S.do("vector", lambda e, xs=xs, pn=pn, i=i, lq=lq: e.tensor_scalar(
                    lq[0:pn, :], xs, st[0:pn, SM + i:SM + i + 1],
                    st[0:pn, SV + i:SV + i + 1], ALU.add, ALU.mult),
                    waits=[a2] + w_src + S.need(f"lnq{i % 2}"))
                d4 = S.do("vector", lambda e: e.drain(), inc=s_dv)
                if kind == "al" and which == 1:
                    S.adddep(f"Rraw_cons{par}", d4)
                if kind == "halo":
                    S.setdep("xh_cons", d4)
                dlist[i] = d4

            emit_apply(0)
            if nch > 1:
                emit_apply(1)
            for i, (kind, k) in enumerate(chunks):
                d4 = dlist[i]
                lq = sb["lnq0"] if i % 2 == 0 else sb["lnq1"]
                lqk = f"lnq{i % 2}"
                wpb = S.need("pbf0")
                tr = None
                if kind == "halo":
                    for c in range(8):
                        tr = S.do("tensor", lambda e, c=c, lq=lq: e.transpose(
                            pbf[:, c * 64:(c + 1) * 64],
                            lq[0:64, c * 128:(c + 1) * 128],
                            sb["idb"][0:64, 0:64]),
                            waits=([d4] + wpb + [idb_w]) if c == 0 else [],
                            inc=s_pe if c == 7 else None)
                    S.setdep(lqk, tr)
                    ev = None
                    for c in range(8):
                        ev = S.do("vector", lambda e, c=c, dst=dst, dstw=dstw: e.tensor_copy(
                            dst[:, dstw * c: dstw * c + 32],
                            pbf[:, 64 * c: 64 * c + 32]),
                            waits=[tr] if c == 0 else [])
                        ev = S.do("vector", lambda e, c=c, dst=dst, dstw=dstw: e.tensor_copy(
                            dst[:, dstw * c + 544: dstw * c + 576],
                            pbf[:, 64 * c + 32: 64 * c + 64]),
                            inc=s_dv if c == 7 else None)
                else:
                    for c in range(8):
                        tr = S.do("tensor", lambda e, c=c, lq=lq: e.transpose(
                            pbf[:, c * 128:(c + 1) * 128],
                            lq[:, c * 128:(c + 1) * 128], sb["idb"][:]),
                            waits=([d4] + wpb + [idb_w]) if c == 0 else [],
                            inc=s_pe if c == 7 else None)
                    S.setdep(lqk, tr)
                    lo = (32 + 128 * k) if which == 1 else 128 * k
                    ev = None
                    for c in range(8):
                        ev = S.do("vector", lambda e, c=c, lo=lo, dst=dst, dstw=dstw: e.tensor_copy(
                            dst[:, dstw * c + lo: dstw * c + lo + 128],
                            pbf[:, c * 128: c * 128 + 128]),
                            waits=[tr] if c == 0 else [],
                            inc=s_dv if c == 7 else None)
                S.setdep("pbf0", ev)
                S.adddep(out_key, ev)
                if i + 2 < nch:
                    emit_apply(i + 2)
                yield

        def proj_qk(tile, w):
            if w == "wq":
                dst, dstw, rhs_lo = sb["qat"], 512, 32
            else:
                dst, dstw, rhs_lo = sb["kfm"], 576, 0
            out_key = "qfmw" if w == "wq" else "kfmw"
            lnw = S.need("ln1w")
            rots = [(0, 1), (2, 3)]
            gcount = [0]
            for grp in _ogroups(8, 2):
                n = len(grp)
                gi = GIDX[(w, 0, grp[0] * 128)]
                gb, slot = load_granule(tile, gi)
                wv = gran_loaded[gi]
                banks = rots[gcount[0] % 2][:n]
                gcount[0] += 1
                bw = S.need(*[bkey(b) for b in banks])
                pe_done = None
                for kc in range(8):
                    for j in range(n):
                        oc = grp[j]
                        pe_done = S.do("tensor",
                                       lambda e, kc=kc, j=j, oc=oc, gi=gi, gb=gb, banks=banks, grp=grp, rhs_lo=rhs_lo: e.matmul(
                                           pbank[banks[j]][:, 0:512],
                                           gslice(gi, gb, kc, (oc - grp[0]) * 128, 128),
                                           sb["ln1"][:, 576 * kc + rhs_lo: 576 * kc + rhs_lo + 512],
                                           start=(kc == 0), stop=(kc == 7)),
                                       waits=([wv] + lnw + bw) if (kc == 0 and j == 0) else [],
                                       inc=s_pe if (kc == 7 and j == n - 1) else None)
                for j in range(n):
                    oc = grp[j]
                    ev = S.do("scalar", lambda e, j=j, oc=oc, banks=banks, dst=dst, dstw=dstw: e.copy(
                        dst[:, dstw * oc: dstw * oc + 512], pbank[banks[j]][:, 0:512]),
                        waits=[pe_done], inc=s_ac)
                    S.setdep(bkey(banks[j]), ev)
                    S.adddep(out_key, ev)
                if w == "wk":
                    banks2 = rots[gcount[0] % 2][:n]
                    gcount[0] += 1
                    bw2 = S.need(*[bkey(b) for b in banks2])
                    pe2 = None
                    for kc in range(8):
                        for j in range(n):
                            oc = grp[j]
                            pe2 = S.do("tensor",
                                       lambda e, kc=kc, j=j, oc=oc, gi=gi, gb=gb, banks2=banks2, grp=grp: e.matmul(
                                           pbank[banks2[j]][:, 0:64],
                                           gslice(gi, gb, kc, (oc - grp[0]) * 128, 128),
                                           sb["ln1"][:, 576 * kc + 512: 576 * kc + 576],
                                           start=(kc == 0), stop=(kc == 7)),
                                       waits=bw2 if (kc == 0 and j == 0) else [],
                                       inc=s_pe if (kc == 7 and j == n - 1) else None)
                    for j in range(n):
                        oc = grp[j]
                        ev = S.do("scalar", lambda e, j=j, oc=oc, banks2=banks2: e.copy(
                            sb["kfm"][:, 576 * oc + 512: 576 * oc + 576],
                            pbank[banks2[j]][:, 0:64]),
                            waits=[pe2], inc=s_ac)
                        S.setdep(bkey(banks2[j]), ev)
                        S.adddep(out_key, ev)
                    slot_free[slot].append(pe2)
                else:
                    slot_free[slot].append(pe_done)
                yield

        def proj_tok(tile, w):
            par = tile % 2
            R = Rbuf(tile)
            nts = 5 if w == "wv" else 4
            bank_i = [0]
            for (o0, ocols) in ((0, 384), (384, 384), (768, 256)):
                gi = GIDX[(w, 0, o0)]
                gb, slot = load_granule(tile, gi)
                wv = gran_loaded[gi]
                last = None
                for ts in range(nts):
                    tw = 64 if (w == "wv" and ts == 4) else 128
                    bk = bank_i[0] % 4
                    bank_i[0] += 1
                    bw = S.need(bkey(bk))
                    if w == "wv":
                        srcw = S.need("ln1w")
                    else:
                        srcw = S.need("qfmw2")
                    for kc in range(8):
                        if w == "wv":
                            def fn(e, kc=kc, ts=ts, tw=tw, bk=bk, gi=gi, gb=gb, ocols=ocols):
                                return e.matmul(
                                    pbank[bk][0:tw, 0:ocols],
                                    sb["ln1"][:, 576 * kc + ts * 128: 576 * kc + ts * 128 + tw],
                                    gslice(gi, gb, kc, 0, ocols),
                                    start=(kc == 0), stop=(kc == 7))
                        else:
                            def fn(e, kc=kc, ts=ts, tw=tw, bk=bk, gi=gi, gb=gb, ocols=ocols):
                                return e.matmul(
                                    pbank[bk][0:tw, 0:ocols],
                                    sb["qat"][:, 512 * kc + ts * 128: 512 * kc + ts * 128 + 128],
                                    gslice(gi, gb, kc, 0, ocols),
                                    start=(kc == 0), stop=(kc == 7))
                        last = S.do("tensor", fn,
                                    waits=([wv] + srcw + bw) if kc == 0 else [],
                                    inc=s_pe if kc == 7 else None)
                    if w == "wv":
                        ev = S.do("scalar", lambda e, ts=ts, o0=o0, ocols=ocols, bk=bk, tw=tw: e.copy(
                            sb["vtm"][0:tw, 1024 * ts + o0: 1024 * ts + o0 + ocols],
                            pbank[bk][0:tw, 0:ocols]),
                            waits=[last], inc=s_ac)
                        S.setdep(bkey(bk), ev)
                        S.adddep("vtmw", ev)
                    else:
                        wfs = S.need("fscWO")
                        t1 = S.do("vector", lambda e, bk=bk, o0=o0, ocols=ocols: e.tensor_tensor(
                            sb["fsc"][:, FSC_WO: FSC_WO + ocols], pbank[bk][:, 0:ocols],
                            sb["bo"][:, o0:o0 + ocols], ALU.add),
                            waits=[last] + wfs + consts_done, inc=s_dv)
                        S.setdep(bkey(bk), t1)
                        S.do("vector", lambda e: e.drain())
                        t2 = S.do("vector", lambda e, ts=ts, o0=o0, ocols=ocols, R=R: e.tensor_tensor(
                            R[:, 1024 * ts + o0: 1024 * ts + o0 + ocols],
                            sb["fsc"][:, FSC_WO: FSC_WO + ocols],
                            R[:, 1024 * ts + o0: 1024 * ts + o0 + ocols],
                            ALU.add), waits=S.need(f"Rraw_cons{par}"), inc=s_dv)
                        S.setdep("fscWO", t2)
                        S.adddep(f"Rmid{par}_{ts}", t2)
                    yield
                slot_free[slot].append(last)

        def sched_attention(tile):
            mask_b = sb["msk"].ap().rearrange("p (u c) -> p u c", u=1).broadcast_to([128, 2, 192])
            kw = S.need("kfmw")
            qw = S.need("qfmw")
            vw = S.need("vtmw")
            ascv = sb["fsc"].ap()[:, FSC_A:FSC_A + 384]
            bursts = []
            for a in range(4):
                for hg in range(8):
                    h0 = 4 * (hg // 2) + (hg % 2)
                    bursts.append((a, hg, h0, h0 + 2))
            NB = len(bursts)
            state = [None] * NB

            def part_sc(b):
                a, hg, h0, h1_ = bursts[b]
                sbk = 4 + (b % 2)
                pw = S.need(bkey(sbk))
                sc = None
                for j, hh in enumerate((h0, h1_)):
                    po = 64 * (hh % 2)
                    sc = S.do("tensor", lambda e, j=j, hh=hh, po=po, a=a, sbk=sbk: e.matmul(
                        pbank[sbk][:, 192 * j: 192 * (j + 1)],
                        sb["qat"][po:po + 64,
                                  512 * (hh // 2) + 128 * a: 512 * (hh // 2) + 128 * a + 128],
                        sb["kfm"][po:po + 64,
                                  576 * (hh // 2) + 128 * a: 576 * (hh // 2) + 128 * a + 192],
                        start=True, stop=True),
                        waits=(pw + kw + qw) if j == 0 else [],
                        inc=s_pe if j == 1 else None)
                wasc = S.need("ascw")
                sm = S.do("vector", lambda e, mask_b=mask_b, ascv=ascv, sbk=sbk: e.tensor_tensor(
                    ascv.rearrange("p (u c) -> p u c", c=192),
                    pbank[sbk].ap()[:, 0:384].rearrange("p (u c) -> p u c", c=192),
                    mask_b, ALU.add),
                    waits=[sc] + wasc + consts_done, inc=s_dv)
                S.setdep(bkey(sbk), sm)
                S.do("vector", lambda e: e.drain())
                S.do("vector", lambda e, ascv=ascv: e.reduce_max(
                    sb["sx"][:, 0:1], ascv, axis=AX.X))
                S.do("vector", lambda e: e.drain())
                S.do("vector", lambda e: e.tensor_scalar(
                    sb["sx"][:, 1:2], sb["sx"][:, 0:1],
                    sb["cv"][:, C_NEGSM:C_NEGSM + 1], None, ALU.mult))
                d3 = S.do("vector", lambda e: e.drain(), inc=s_dv)
                act_set("exp")
                a_exp = None
                for u in range(2):
                    a_exp = S.do("scalar", lambda e, u=u, ascv=ascv: e.activation(
                        sb["apb"][:, 192 * u: 192 * (u + 1)],
                        ascv[:, 192 * u: 192 * (u + 1)], AF.Exp,
                        bias=sb["sx"][:, 1:2],
                        scale=sb["cv"][:, C_SMSCALE:C_SMSCALE + 1],
                        accum_out=sb["sx"][:, 2 + u:3 + u]),
                        waits=([d3] + S.need("apbw")) if u == 0 else [],
                        inc=s_ac if u == 1 else None)
                S.setdep("ascw", a_exp)
                S.do("vector", lambda e: e.reciprocal(
                    sb["sx"][:, 4:6], sb["sx"][:, 2:4]), waits=[a_exp])
                S.do("vector", lambda e: e.drain())
                r_b = sb["sx"].ap().rearrange("p (u c) -> p u c", c=1)[:, 4:6, :].broadcast_to([128, 2, 192])
                pr = (b % 2) * 384
                S.do("vector", lambda e, r_b=r_b, pr=pr: e.tensor_tensor(
                    sb["pnb"].ap()[:, pr:pr + 384].rearrange("p (u c) -> p u c", c=192),
                    sb["apb"].ap().rearrange("p (u c) -> p u c", c=192),
                    r_b, ALU.mult), waits=S.need(f"pnbw{b % 2}"))
                d_pn = S.do("vector", lambda e: e.drain(), inc=s_dv)
                S.setdep("apbw", d_pn)
                state[b] = d_pn

            def part_tr(b):
                d_pn = state[b]
                reg = (b % 2) * 512
                pr = (b % 2) * 384
                wpb = S.need("pbf0")
                tr = None
                for j in range(2):
                    tr = S.do("tensor", lambda e, j=j, pr=pr: e.transpose(
                        pbf[:, 256 * j: 256 * j + 128],
                        sb["pnb"][:, pr + 192 * j: pr + 192 * j + 128], sb["idb"][:]),
                        waits=([d_pn] + wpb + [idb_w]) if j == 0 else [])
                    tr = S.do("tensor", lambda e, j=j, pr=pr: e.transpose(
                        pbf[0:64, 256 * j + 128: 256 * j + 256],
                        sb["pnb"][:, pr + 192 * j + 128: pr + 192 * j + 192], sb["idb"][:]),
                        inc=s_pe if j == 1 else None)
                S.setdep(f"pnbw{b % 2}", tr)
                evp = S.do("vector", lambda e, reg=reg: e.tensor_copy(
                    sb["pts"][:, reg: reg + 512], pbf[:, 0:512]),
                    waits=[tr] + S.need(f"ptsw{reg}"), inc=s_dv)
                S.setdep("pbf0", evp)
                state[b] = evp

            def part_pv(b):
                a, hg, h0, h1_ = bursts[b]
                evp = state[b]
                reg = (b % 2) * 512
                pv_last = None
                pw6 = S.need("pb6")
                for j, hh in enumerate((h0, h1_)):
                    pv_last = S.do("tensor",
                                   lambda e, hh=hh, a=a, reg=reg, j=j: e.matmul(
                                       pbank[6][0:64, 128 * j: 128 * j + 128],
                                       sb["vtm"][:, 1024 * a + 64 * hh: 1024 * a + 64 * hh + 64],
                                       sb["pts"][:, reg + 256 * j: reg + 256 * j + 128],
                                       start=True, stop=False),
                                   waits=([evp] + vw + pw6) if j == 0 else [])
                    pv_last = S.do("tensor",
                                   lambda e, hh=hh, a=a, reg=reg, j=j: e.matmul(
                                       pbank[6][0:64, 128 * j: 128 * j + 128],
                                       sb["vtm"][0:64, 1024 * (a + 1) + 64 * hh: 1024 * (a + 1) + 64 * hh + 64],
                                       sb["pts"][0:64, reg + 256 * j + 128: reg + 256 * j + 256],
                                       start=False, stop=True),
                                   inc=s_pe if j == 1 else None)
                S.setdep(f"ptsw{reg}", pv_last)
                ev = None
                for j, hh in enumerate((h0, h1_)):
                    ev = S.do("scalar", lambda e, hh=hh, a=a, j=j: e.copy(
                        sb["qat"][64 * (hh % 2): 64 * (hh % 2) + 64,
                                  512 * (hh // 2) + 128 * a: 512 * (hh // 2) + 128 * a + 128],
                        pbank[6][0:64, 128 * j: 128 * j + 128]),
                        waits=[pv_last] if j == 0 else [], inc=s_ac if j == 1 else None)
                S.setdep("pb6", ev)
                S.adddep("qfmw2", ev)

            for bb in range(0, NB + 4, 2):
                for b in (bb, bb + 1):
                    if 4 <= b < NB + 4:
                        part_pv(b - 4)
                for b in (bb, bb + 1):
                    if 2 <= b < NB + 2:
                        part_tr(b - 2)
                for b in (bb, bb + 1):
                    if b < NB:
                        part_sc(b)
                yield

        def sched_ffn_in(tile):
            ln2w = S.need("ln2w")
            S.dep.pop("ln2w", None)
            for grp in _ogroups(32, 2):
                n = len(grp)
                banksA = (0, 1)[:n]
                banksB = (2, 3)[:n]
                gi1 = GIDX[("w1", 0, grp[0] * 128)]
                gb1, slot1 = load_granule(tile, gi1)
                wv1 = gran_loaded[gi1]
                bwA = S.need(*[bkey(b) for b in banksA])
                lastA = None
                for kc in range(8):
                    for j in range(n):
                        oc = grp[j]
                        lastA = S.do("tensor",
                                     lambda e, kc=kc, j=j, oc=oc, gi1=gi1, gb1=gb1, grp=grp, banksA=banksA: e.matmul(
                                         pbank[banksA[j]][:, 0:512],
                                         gslice(gi1, gb1, kc, (oc - grp[0]) * 128, 128),
                                         sb["ln2"][:, 512 * kc: 512 * (kc + 1)],
                                         start=(kc == 0), stop=(kc == 7)),
                                     waits=([wv1] + ln2w + bwA) if (kc == 0 and j == 0) else [],
                                     inc=s_pe if (kc == 7 and j == n - 1) else None)
                slot_free[slot1].append(lastA)
                act_set("silu")
                wfs = S.need("fsc")
                ev_si = None
                for j in range(n):
                    oc = grp[j]
                    ev_si = S.do("scalar", lambda e, j=j, oc=oc, banksA=banksA: e.activation(
                        sb["fsc"][:, 512 * j: 512 * (j + 1)], pbank[banksA[j]][:, 0:512],
                        AF.Silu, scale=sb["s1"][:, oc:oc + 1]),
                        waits=([lastA] + wfs + consts_done) if j == 0 else [], inc=s_ac)
                for j in range(n):
                    S.setdep(bkey(banksA[j]), ev_si)
                gi2 = GIDX[("w2", 0, grp[0] * 128)]
                gb2, slot2 = load_granule(tile, gi2)
                wv2 = gran_loaded[gi2]
                bwB = S.need(*[bkey(b) for b in banksB])
                lastB = None
                for kc in range(8):
                    for j in range(n):
                        oc = grp[j]
                        lastB = S.do("tensor",
                                     lambda e, kc=kc, j=j, oc=oc, gi2=gi2, gb2=gb2, grp=grp, banksB=banksB: e.matmul(
                                         pbank[banksB[j]][:, 0:512],
                                         gslice(gi2, gb2, kc, (oc - grp[0]) * 128, 128),
                                         sb["ln2"][:, 512 * kc: 512 * (kc + 1)],
                                         start=(kc == 0), stop=(kc == 7)),
                                     waits=([wv2] + bwB) if (kc == 0 and j == 0) else [],
                                     inc=s_pe if (kc == 7 and j == n - 1) else None)
                slot_free[slot2].append(lastB)
                d_g = None
                for j in range(n):
                    oc = grp[j]
                    d_g = S.do("vector",
                               lambda e, j=j, oc=oc, banksB=banksB: e.scalar_tensor_tensor(
                                   sb["g"][:, 512 * oc: 512 * (oc + 1)],
                                   pbank[banksB[j]][:, 0:512],
                                   sb["s2"][:, oc:oc + 1],
                                   sb["fsc"][:, 512 * j: 512 * (j + 1)],
                                   ALU.mult, ALU.mult),
                               waits=([lastB, ev_si] + consts_done) if j == 0 else [],
                               inc=s_dv)
                for j in range(n):
                    S.setdep(bkey(banksB[j]), d_g)
                S.setdep("fsc", d_g)
                S.adddep("gw", d_g)
                yield

        def sched_h(tile, w, src_name, dst_name, scale_name):
            srcw = S.need(src_name + "w")
            S.dep.pop(src_name + "w", None)
            ab = [0]
            for grp in _ogroups(32, 2):
                n = len(grp)
                banks = ((0, 1) if ab[0] % 2 == 0 else (2, 3))[:n]
                ab[0] += 1
                bw = S.need(*[bkey(b) for b in banks])
                last = None
                for kq in range(4):
                    gi = GIDX[(w, kq * 8, grp[0] * 128)]
                    gb, slot = load_granule(tile, gi)
                    wv = gran_loaded[gi]
                    for kc in range(8):
                        kcg = kq * 8 + kc
                        for j in range(n):
                            oc = grp[j]
                            last = S.do("tensor",
                                        lambda e, kc=kc, kcg=kcg, j=j, oc=oc, gi=gi, gb=gb, grp=grp, banks=banks, src_name=src_name: e.matmul(
                                            pbank[banks[j]][:, 0:512],
                                            gslice(gi, gb, kc, (oc - grp[0]) * 128, 128),
                                            sb[src_name][:, 512 * kcg: 512 * (kcg + 1)],
                                            start=(kcg == 0), stop=(kcg == 31)),
                                        waits=([wv] + srcw + bw) if (kc == 0 and j == 0) else [],
                                        inc=s_pe if (kc == 7 and j == n - 1) else None)
                    slot_free[slot].append(last)
                act_set("gelu")
                ev = None
                for j in range(n):
                    oc = grp[j]
                    ev = S.do("scalar",
                              lambda e, j=j, oc=oc, banks=banks, dst_name=dst_name, scale_name=scale_name: e.activation(
                                  sb[dst_name][:, 512 * oc: 512 * (oc + 1)],
                                  pbank[banks[j]][:, 0:512],
                                  AF.Gelu, scale=sb[scale_name][:, oc:oc + 1]),
                              waits=([last] + consts_done) if j == 0 else [], inc=s_ac)
                for j in range(n):
                    S.setdep(bkey(banks[j]), ev)
                S.adddep(dst_name + "w", ev)
                yield

        def sched_w3(tile):
            par = tile % 2
            R = Rbuf(tile)
            hw = S.need("gw")
            S.dep.pop("gw", None)
            bank_i = [0]
            for oq in range(4):
                gis = [GIDX[("w3", kh * 16, oq * 256)] for kh in range(2)]
                gbslots = [load_granule(tile, gi) for gi in gis]
                gbs = [t[0] for t in gbslots]
                slots = [t[1] for t in gbslots]
                wvs = [gran_loaded[gi] for gi in gis]
                for ts in range(4):
                    bk = bank_i[0] % 4
                    bank_i[0] += 1
                    bw = S.need(bkey(bk))
                    last = None
                    for kh in range(2):
                        for kc in range(16):
                            kcg = kh * 16 + kc
                            last = S.do("tensor",
                                        lambda e, ts=ts, kh=kh, kc=kc, kcg=kcg, bk=bk, oq=oq, gis=gis, gbs=gbs: e.matmul(
                                            pbank[bk][:, 0:256],
                                            sb["g"][:, 512 * kcg + 128 * ts: 512 * kcg + 128 * ts + 128],
                                            gslice(gis[kh], gbs[kh], kc, 0, 256),
                                            start=(kcg == 0), stop=(kcg == 31)),
                                        waits=(wvs + hw + bw) if kcg == 0 else [],
                                        inc=s_pe if kcg == 31 else None)
                    wfs = S.need("fscW3")
                    t1 = S.do("vector", lambda e, bk=bk, oq=oq: e.tensor_tensor(
                        sb["fsc"][:, FSC_W3: FSC_W3 + 256], pbank[bk][:, 0:256],
                        sb["w3s"][:, 256 * oq: 256 * (oq + 1)], ALU.mult),
                        waits=[last] + wfs + consts_done, inc=s_dv)
                    S.setdep(bkey(bk), t1)
                    S.do("vector", lambda e: e.drain())
                    t2 = S.do("vector", lambda e, ts=ts, oq=oq, R=R: e.tensor_tensor(
                        R[:, 1024 * ts + 256 * oq: 1024 * ts + 256 * oq + 256],
                        sb["fsc"][:, FSC_W3: FSC_W3 + 256],
                        R[:, 1024 * ts + 256 * oq: 1024 * ts + 256 * oq + 256],
                        ALU.add), inc=s_dv)
                    S.setdep("fscW3", t2)
                    S.adddep(f"Rfin{par}", t2)
                    yield
                sv = (s_pe, s_pe.n)
                for sl in slots:
                    slot_free[sl].append(sv)

        def load_R(tile):
            par = tile % 2
            R = Rbuf(tile)
            seq, s0 = tile // 4, T * (tile % 4)
            base = seq * L + s0
            wfree = S.need(f"Rfree{par}")
            for k in range(4):
                sv = S.do("scalar", lambda e, base=base, R=R, k=k: e.dma_start(
                    R[:, 1024 * k: 1024 * (k + 1)],
                    x_in[base + 128 * k: base + 128 * k + 128, :]),
                    waits=wfree, inc=s_x, inck=16)
                S.setdep(f"Rw{par}_{k}", sv)

        def load_xh(tile):
            seq, s0 = tile // 4, T * (tile % 4)
            base = seq * L
            waits = S.need("xh_cons")
            S.dep.pop("xhw", None)
            if tile % 4 == 0:
                mz = S.do("gpsimd", lambda e: e.memset(sb["xh"][0:32, :], 0.0),
                          waits=waits, inc=s_pl)
                S.adddep("xhw", mz)
            else:
                sv = S.do("scalar", lambda e, base=base, s0=s0: e.dma_start(
                    sb["xh"][0:32, :], x_in[base + s0 - 32: base + s0, :]),
                    waits=waits, inc=s_x, inck=16)
                S.adddep("xhw", sv)
            if tile % 4 == 3:
                mz = S.do("gpsimd", lambda e: e.memset(sb["xh"][32:64, :], 0.0),
                          waits=waits, inc=s_pl)
                S.adddep("xhw", mz)
            else:
                sv = S.do("scalar", lambda e, base=base, s0=s0: e.dma_start(
                    sb["xh"][32:64, :], x_in[base + s0 + 512: base + s0 + 544, :]),
                    waits=waits, inc=s_x, inck=16)
                S.adddep("xhw", sv)

        def store_out(tile):
            par = tile % 2
            R = Rbuf(tile)
            seq, s0 = tile // 4, T * (tile % 4)
            base = seq * L + s0
            wsrc = S.need(f"Rfin{par}")
            S.dep.pop(f"Rfin{par}", None)
            sv = S.do("scalar", lambda e, base=base, R=R: e.dma_start(
                y_out[base: base + 512, :].rearrange("(ts p) d -> p ts d", p=128),
                R.ap().rearrange("p (ts d) -> p ts d", d=1024)),
                waits=wsrc, inc=s_o, inck=16)
            S.setdep(f"Rfree{par}", sv)

        def dbg_snap(tile, name, buf, depkeys):
            if tile not in DBG_TILES:
                return
            key = f"dbg_{name}_t{tile}"
            ap_shape = [128, buf.shape[1]]
            dt = F32
            dst = nc.dram_tensor(key, ap_shape, dt, kind="ExternalOutput").ap()
            dbg_out[key] = dst
            w = S.need(*depkeys)
            S.do("gpsimd", lambda e, dst=dst, buf=buf: e.dma_start(dst, buf[:, :]),
                 waits=w, inc=s_o, inck=16)

        def bg_gen(tile):
            par = tile % 2
            load_R(tile)
            load_xh(tile)
            yield
            for _ in sched_ln(tile, 1):
                yield
            for _ in proj_qk(tile, "wq"):
                yield
            dbg_snap(tile, "q", sb["qat"], ("qfmw",))
            for _ in proj_qk(tile, "wk"):
                yield
            dbg_snap(tile, "k", sb["kfm"], ("kfmw",))
            for _ in proj_tok(tile, "wv"):
                yield
            dbg_snap(tile, "v", sb["vtm"], ("vtmw",))
            dbg_snap(tile, "ln1", sb["ln1"], ("ln1w",))
            S.dep.pop("ln1w", None)
            for _ in sched_attention(tile):
                yield
            dbg_snap(tile, "attn", sb["qat"], ("qfmw2",))
            S.dep.pop("qfmw", None)
            S.dep.pop("kfmw", None)
            S.dep.pop("vtmw", None)
            for _ in proj_tok(tile, "wo"):
                yield
            S.dep.pop("qfmw2", None)
            S.dep.pop(f"Rraw_cons{par}", None)
            for _ in sched_ln(tile, 2):
                yield
            dbg_snap(tile, "ln2", sb["ln2"], ("ln2w",))
            for k in range(4):
                S.dep.pop(f"Rmid{par}_{k}", None)

        def fg_gen(tile):
            for _ in sched_ffn_in(tile):
                yield
            dbg_snap(tile, "g", sb["g"], ("gw",))
            for _ in sched_h(tile, "h1", "g", "h", "sh1"):
                yield
            dbg_snap(tile, "h", sb["h"], ("hw",))
            for _ in sched_h(tile, "h2", "h", "g", "sh2"):
                yield
            for _ in sched_w3(tile):
                yield

        FG_STEPS = 16 + 16 + 16 + 16
        BG_STEPS = 1 + 9 + 4 + 4 + 15 + 18 + 12 + 6
        PACE = 1.15

        for _ in bg_gen(0):
            pass
        for tile in range(NTILE):
            fg = fg_gen(tile)
            bg = bg_gen(tile + 1) if tile + 1 < NTILE else None
            fi = bi = 0
            for _ in fg:
                fi += 1
                if bg is not None:
                    tgt = min(BG_STEPS, int(BG_STEPS * PACE * fi / FG_STEPS))
                    while bi < tgt:
                        try:
                            next(bg)
                            bi += 1
                        except StopIteration:
                            bg = None
                            break
            while bg is not None:
                try:
                    next(bg)
                    bi += 1
                except StopIteration:
                    bg = None
            store_out(tile)

        if TRUNC is None:
            S.do("scalar", lambda e: e.nop(), waits=[(s_o, s_o.n)])
        print("SCHED total ops:", S.seq)

        block = E(nc.Block())

        @block.tensor
        def _(t):
            S.replay("tensor", t)

        @block.vector
        def _(v):
            S.replay("vector", v)

        @block.scalar
        def _(s):
            S.replay("scalar", s)

        @block.gpsimd
        def _(g):
            S.replay("gpsimd", g)

        @block.sync
        def _(sy):
            S.replay("sync", sy)

    return nc


def _host_inputs(inputs):
    g = {k: np.asarray(v) for k, v in inputs.items()}
    asc = np.ascontiguousarray
    base = {
        "wqT": asc(g["L0_attn_Wq"].T), "wkT": asc(g["L0_attn_Wk"].T),
        "wvT": asc(g["L0_attn_Wv"].T), "woT": asc(g["L0_attn_Wo"].T),
        "w1T": asc(g["L0_ffn_w1_w"].T), "w2T": asc(g["L0_ffn_w2_w"].T),
        "h1T": asc(g["L0_ffn_h1_w"].T), "h2T": asc(g["L0_ffn_h2_w"].T),
        "w3T": asc(g["L0_ffn_w3_w"].T),
        "s1": asc(g["L0_ffn_w1_s"].reshape(32, 128).T),
        "s2": asc(g["L0_ffn_w2_s"].reshape(32, 128).T),
        "sh1": asc(g["L0_ffn_h1_s"].reshape(32, 128).T),
        "sh2": asc(g["L0_ffn_h2_s"].reshape(32, 128).T),
        "bo_bc": asc(np.broadcast_to(g["L0_attn_bo"], (128, D))),
        "w3s_bc": asc(np.broadcast_to(g["L0_ffn_w3_s"], (128, D))),
    }
    r = np.arange(128)[:, None]
    c = np.arange(192)[None, :]
    base["mask"] = np.where((c >= r) & (c < r + WINDOW), 0.0, -1e30).astype(np.float32)
    base["ident"] = np.eye(128, dtype=np.float32)
    cv = np.zeros((128, 8), np.float32)
    cv[:, C_NEGINVD] = -1.0 / D
    cv[:, C_INVD] = 1.0 / D
    cv[:, C_EPS] = 1e-5
    cv[:, C_NEGHALF] = -0.5
    cv[:, C_SMSCALE] = 0.125
    cv[:, C_NEGSM] = -0.125
    base["cvec"] = cv
    x = g["L0_attn_x"].reshape(16, L, D)
    in_maps = []
    for core in range(NCORE):
        m = dict(base)
        m["x"] = asc(x[2 * core: 2 * core + 2].reshape(NSEQ * L, D))
        in_maps.append(m)
    return in_maps


def kernel(**inputs):
    in_maps = _host_inputs(inputs)
    nc = build_program()
    res = run_bass_kernel_spmd(nc, in_maps, core_ids=list(range(NCORE)))
    out = np.concatenate(
        [np.asarray(r["y"]).reshape(NSEQ, L, D) for r in res.results], axis=0)
    return out.astype(np.float32)


# version 14
# speedup vs baseline: 1.0878x; 1.0878x over previous
import numpy as np
from contextlib import ExitStack

import concourse.bass as bass
import concourse.mybir as mybir
from concourse.bass_utils import run_bass_kernel_spmd

F32 = mybir.dt.float32
BF16 = mybir.dt.bfloat16
AF = mybir.ActivationFunctionType
ALU = mybir.AluOpType
AX = mybir.AxisListType

TRUNC = None
DBG_TILES = ()
D, H, DHEAD, WINDOW, PADW = 1024, 16, 64, 64, 32
L, T, DHID = 2048, 512, 4096
NSEQ, NTILE, NCORE = 2, 8, 8

C_NEGINVD, C_INVD, C_EPS, C_NEGHALF, C_SMSCALE, C_NEGSM = 0, 1, 2, 3, 4, 5

WMETA = {
    "wq": (D, D), "wk": (D, D), "wv": (D, D), "wo": (D, D),
    "w1": (D, DHID), "w2": (D, DHID),
    "h1": (DHID, DHID), "h2": (DHID, DHID),
    "w3": (DHID, D),
}


def _ogroups(noc, g):
    out, i = [], 0
    while i < noc:
        n = min(g, noc - i)
        out.append(list(range(i, i + n)))
        i += n
    return out


def granule_plan():
    plan = []
    for w in ("wq", "wk"):
        for grp in _ogroups(8, 2):
            plan.append((w, 0, 8, grp[0] * 128, len(grp) * 128))
    for w in ("wv",):
        for (o0, oc) in ((0, 384), (384, 384), (768, 256)):
            plan.append((w, 0, 8, o0, oc))
    for w in ("wo",):
        for (o0, oc) in ((0, 384), (384, 384), (768, 256)):
            plan.append((w, 0, 8, o0, oc))
    for w in ("w1", "w2"):
        for grp in _ogroups(32, 2):
            plan.append((w, 0, 8, grp[0] * 128, len(grp) * 128))
    for w in ("h1", "h2"):
        for grp in _ogroups(32, 2):
            for kq in range(4):
                plan.append((w, kq * 8, 8, grp[0] * 128, len(grp) * 128))
    for oq in range(4):
        for kh in range(2):
            plan.append(("w3", kh * 16, 16, oq * 256, 256))
    return plan


GRANS = granule_plan()
NGRAN = len(GRANS)
GIDX = {(w, kc0, o0): i for i, (w, kc0, nkc, o0, nc_) in enumerate(GRANS)}
BG_WNAMES = ("wq", "wk", "wv", "wo")
SLOT_BASES = [0, 4096, 8192, 12288, 15360]
FG_SLOTS = (0, 1, 2)
BG_SLOTS = (3, 4)
WSL_COLS = 18432

FSC_F = 0
FSC_A = 1024
FSC_W3 = 1408
FSC_WO = 1664
FSC_COLS = 2048


class Sem:
    def __init__(self, handle, owner):
        self.h = handle
        self.owner = owner
        self.n = 0


class Sched:
    ENGINES = ("tensor", "vector", "scalar", "gpsimd", "sync")

    def __init__(self):
        self.ops = {e: [] for e in self.ENGINES}
        self.waited = {e: {} for e in self.ENGINES}
        self.dep = {}
        self.seq = 0

    def do(self, eng, fn, waits=(), inc=None, inck=1):
        import sys as _sys
        self.last_line = _sys._getframe(1).f_lineno
        real = []
        w = self.waited[eng]
        for sv in waits:
            if sv is None:
                continue
            s, v = sv
            if v <= 0 or s.owner == eng:
                continue
            if w.get(id(s), -1) >= v:
                continue
            real.append((s, v))
            w[id(s)] = v
        best = {}
        for (s, v) in real:
            if v > best.get(id(s), (None, -1))[1]:
                best[id(s)] = (s, v)
        real = list(best.values())
        res = None
        if inc is not None:
            inc.n += inck
            res = (inc, inc.n)
        self.ops[eng].append((real, fn, inc, inck, self.seq))
        self.trace = getattr(self, "trace", [])
        self.trace.append((self.seq, eng, self.last_line,
                           [(id(s) % 1000, v) for (s, v) in real],
                           (id(inc) % 1000 if inc else None, inc.n if inc else None)))
        self.seq += 1
        return res

    def need(self, *keys):
        out = []
        for k in keys:
            out.extend(self.dep.get(k, ()))
        return out

    def setdep(self, key, sv):
        if sv is not None:
            self.dep[key] = [sv]

    def adddep(self, key, sv):
        if sv is not None:
            self.dep.setdefault(key, []).append(sv)

    def replay(self, eng_name, engine):
        for (waits, fn, inc, inck, seq) in self.ops[eng_name]:
            if TRUNC is not None and seq >= TRUNC:
                continue
            for (s, v) in waits:
                engine.wait_ge(s.h, v)
            ins = fn(engine)
            if inc is not None:
                ins.then_inc(inc.h, inck)


def build_program():
    nc = bass.Bass("TRN2", target_bir_lowering=False)

    x_in = nc.dram_tensor("x", [NSEQ * L, D], F32, kind="ExternalInput").ap()
    wt = {w: nc.dram_tensor(w + "T", [k, o], F32, kind="ExternalInput").ap()
          for w, (k, o) in WMETA.items()}
    sc_in = {n: nc.dram_tensor(n, [128, 32], F32, kind="ExternalInput").ap()
             for n in ("s1", "s2", "sh1", "sh2")}
    bo_in = nc.dram_tensor("bo_bc", [128, D], F32, kind="ExternalInput").ap()
    w3s_in = nc.dram_tensor("w3s_bc", [128, D], F32, kind="ExternalInput").ap()
    mask_in = nc.dram_tensor("mask", [128, 192], F32, kind="ExternalInput").ap()
    id_in = nc.dram_tensor("ident", [128, 128], F32, kind="ExternalInput").ap()
    cv_in = nc.dram_tensor("cvec", [128, 8], F32, kind="ExternalInput").ap()
    y_out = nc.dram_tensor("y", [NSEQ * L, D], F32, kind="ExternalOutput").ap()
    dbg_out = {}

    scr = [nc.dram_tensor(f"scr{i}", [128, nkc * ncols], BF16, kind="Internal").ap()
           for i, (w, kc0, nkc, o0, ncols) in enumerate(GRANS)]

    ctx = ExitStack()
    E = ctx.enter_context
    with ctx:
        sb = {}
        sb["lnq0"] = E(nc.sbuf_tensor("lnq0", [128, D], BF16))
        sb["lnq1"] = E(nc.sbuf_tensor("lnq1", [128, D], BF16))
        sb["sqd"] = sb["lnq1"]
        sb["ln1"] = E(nc.sbuf_tensor("ln1", [128, 8 * 576], BF16))
        sb["ln2"] = E(nc.sbuf_tensor("ln2", [128, 8 * 512], BF16))
        sb["qat"] = E(nc.sbuf_tensor("qat", [128, 8 * 512], BF16))
        sb["kfm"] = E(nc.sbuf_tensor("kfm", [128, 8 * 576], BF16))
        sb["vtm"] = E(nc.sbuf_tensor("vtm", [128, 5 * 1024], BF16))
        sb["g"] = E(nc.sbuf_tensor("g", [128, 32 * 512], BF16))
        sb["h"] = E(nc.sbuf_tensor("h", [128, 32 * 512], BF16))
        sb["apb"] = E(nc.sbuf_tensor("apb", [128, 384], BF16))
        sb["pnb"] = E(nc.sbuf_tensor("pnb", [128, 768], BF16))
        sb["pts"] = E(nc.sbuf_tensor("pts", [128, 1024], BF16))
        sb["idb"] = E(nc.sbuf_tensor("idb", [128, 128], BF16))
        sb["wsl"] = E(nc.sbuf_tensor("wsl", [128, WSL_COLS], BF16))
        sb["R0"] = E(nc.sbuf_tensor("R0", [128, 4 * 1024], F32))
        sb["R1"] = E(nc.sbuf_tensor("R1", [128, 4 * 1024], F32))
        sb["xh"] = E(nc.sbuf_tensor("xh", [64, 1024], F32))
        sb["fsc"] = E(nc.sbuf_tensor("fsc", [128, FSC_COLS], F32))
        sb["bo"] = E(nc.sbuf_tensor("bo", [128, D], F32))
        sb["w3s"] = E(nc.sbuf_tensor("w3s", [128, D], F32))
        sb["msk"] = E(nc.sbuf_tensor("msk", [128, 192], F32))
        sb["cv"] = E(nc.sbuf_tensor("cv", [128, 8], F32))
        for n in ("s1", "s2", "sh1", "sh2"):
            sb[n] = E(nc.sbuf_tensor("sb_" + n, [128, 32], F32))
        sb["st"] = E(nc.sbuf_tensor("st", [128, 32], F32))
        sb["sx"] = E(nc.sbuf_tensor("sx", [128, 8], F32))

        pbank = [E(nc.psum_tensor(f"pb{i}", [128, 512], F32)) for i in range(7)]
        pbf = E(nc.psum_tensor("pbf0", [128, 1024], BF16))

        s_pe = Sem(E(nc.semaphore("s_pe")), "tensor")
        s_dv = Sem(E(nc.semaphore("s_dv")), "vector")
        s_ac = Sem(E(nc.semaphore("s_ac")), "scalar")
        s_pl = Sem(E(nc.semaphore("s_pl")), "gpsimd")
        s_wg = Sem(E(nc.semaphore("s_wg")), "qg")
        s_ws = Sem(E(nc.semaphore("s_ws")), "qs")
        s_cv = Sem(E(nc.semaphore("s_cv")), "qs")
        s_x = Sem(E(nc.semaphore("s_x")), "qa")
        s_o = Sem(E(nc.semaphore("s_o")), "qa")

        S = Sched()

        def Rbuf(tile):
            return sb["R0"] if tile % 2 == 0 else sb["R1"]

        def ldc(dst, src):
            return S.do("scalar", lambda e, d=dst, s=src: e.dma_start(d, s),
                        inc=s_x, inck=16)
        for n in ("s1", "s2", "sh1", "sh2"):
            ldc(sb[n][:], sc_in[n])
        ldc(sb["bo"][:], bo_in)
        ldc(sb["w3s"][:], w3s_in)
        ldc(sb["msk"][:], mask_in)
        ldc(sb["cv"][:], cv_in)
        idb_w = S.do("gpsimd", lambda e: e.dma_start(sb["idb"][:], id_in),
                     inc=s_wg, inck=16)
        consts_done = [(s_x, s_x.n), idb_w]

        slot_free = [[] for _ in range(len(SLOT_BASES))]
        slot_ctr = {"fg": 0, "bg": 0}
        gran_loaded = [None] * NGRAN
        scr_written = [None] * NGRAN

        def load_granule(tile, gi):
            w, kc0, nkc, o0, ncols = GRANS[gi]
            pool = "bg" if w in BG_WNAMES else "fg"
            slots = BG_SLOTS if pool == "bg" else FG_SLOTS
            slot = slots[slot_ctr[pool] % len(slots)]
            slot_ctr[pool] += 1
            base = SLOT_BASES[slot]
            dst = sb["wsl"][:, base: base + nkc * ncols]
            waits = list(slot_free[slot])
            slot_free[slot] = []
            if tile == 0:
                src = wt[w].rearrange("(kc p) o -> p kc o", p=128)[:, kc0:kc0 + nkc, o0:o0 + ncols]
                sv = S.do("gpsimd", lambda e, d=dst, s=src: e.dma_start(d, s),
                          waits=waits, inc=s_wg, inck=16)
                gran_loaded[gi] = sv
                wv = S.do("sync", lambda e, d=scr[gi], s=dst: e.dma_start(d, s),
                    waits=[sv], inc=s_cv, inck=16)
                scr_written[gi] = wv
                slot_free[slot].append(wv)
            else:
                q = "gpsimd" if pool == "bg" else "sync"
                qsem = s_wg if pool == "bg" else s_ws
                sv = S.do(q, lambda e, d=dst, s=scr[gi]: e.dma_start(d, s),
                          waits=waits + [scr_written[gi]], inc=qsem, inck=16)
                gran_loaded[gi] = sv
            return base, slot

        def gslice(gi, base, kc_local, o_local, width):
            w, kc0, nkc, o0, ncols = GRANS[gi]
            off = base + kc_local * ncols + o_local
            return sb["wsl"][:, off: off + width]

        def bkey(i):
            return f"pb{i}"

        act_state = ["?"]

        def act_set(name):
            want = {"exp": 6, "ln": 6, "silu": 18, "gelu": 10}[name]
            if act_state[0] == want:
                return
            act_state[0] = want
            S.do("scalar", lambda e: e.drain())
            if want == 6:
                def loadset6(e):
                    return e.add_instruction(mybir.InstLoadActFuncSet(
                        name=e.bass.get_next_instruction_name(),
                        act_func_set_id=6, ins=[], outs=[]))
                S.do("scalar", loadset6)
            else:
                f = {18: AF.Silu, 10: AF.Gelu}[want]
                S.do("scalar", lambda e, f=f: e.preload_act_table(f))
            S.do("scalar", lambda e: e.drain())

        SA, SB, SM, SQ, SV, SL = 0, 5, 10, 15, 20, 25

        def sched_ln(tile, which):
            par = tile % 2
            R = Rbuf(tile)
            if which == 1:
                dst, dstw, out_key = sb["ln1"], 576, "ln1w"
                chunks = [("halo", None)] + [("al", k) for k in range(4)]
            else:
                dst, dstw, out_key = sb["ln2"], 512, "ln2w"
                chunks = [("al", k) for k in range(4)]
            st = sb["st"]
            nch = len(chunks)

            def srcof(kind, k):
                if kind == "halo":
                    return 64, sb["xh"][0:64, :], S.need("xhw")
                if which == 1:
                    return 128, R[:, k * 1024: (k + 1) * 1024], S.need(f"Rw{par}_{k}")
                return 128, R[:, k * 1024: (k + 1) * 1024], S.need(f"Rmid{par}_{k}")

            a1 = None
            d1 = None
            for i, (kind, k) in enumerate(chunks):
                pn, xs, w_src = srcof(kind, k)
                d1 = S.do("vector", lambda e, xs=xs, pn=pn, i=i: e.reduce_sum(
                    st[0:pn, SA + i:SA + i + 1], xs, axis=AX.X),
                    waits=w_src, inc=s_dv)
                a1 = S.do("scalar", lambda e, xs=xs, pn=pn, i=i: e.activation(
                    sb["sqd"][0:pn, :], xs, AF.Square,
                    accum_out=st[0:pn, SB + i:SB + i + 1]),
                    waits=w_src + S.need("lnq1"), inc=s_ac)
            yield
            dA = S.do("vector", lambda e: e.drain(), inc=s_dv)
            S.do("vector", lambda e, nch=nch: e.tensor_scalar(
                st[:, SM:SM + nch], st[:, SA:SA + nch],
                sb["cv"][:, C_NEGINVD:C_NEGINVD + 1], None, ALU.mult),
                waits=consts_done)
            S.do("vector", lambda e, nch=nch: e.tensor_scalar(
                st[:, SQ:SQ + nch], st[:, SB:SB + nch],
                sb["cv"][:, C_INVD:C_INVD + 1], None, ALU.mult),
                waits=[a1])
            S.do("vector", lambda e: e.drain())
            S.do("vector", lambda e, nch=nch: e.tensor_tensor(
                st[:, SV:SV + nch], st[:, SM:SM + nch], st[:, SM:SM + nch],
                ALU.mult))
            S.do("vector", lambda e: e.drain())
            S.do("vector", lambda e, nch=nch: e.tensor_tensor(
                st[:, SV:SV + nch], st[:, SQ:SQ + nch], st[:, SV:SV + nch],
                ALU.subtract))
            d3 = S.do("vector", lambda e: e.drain(), inc=s_dv)
            act_set("ln")
            S.do("scalar", lambda e, nch=nch: e.activation(
                st[:, SL:SL + nch], st[:, SV:SV + nch], AF.Ln,
                bias=sb["cv"][:, C_EPS:C_EPS + 1]), waits=[d3])
            a2 = S.do("scalar", lambda e, nch=nch: e.activation(
                st[:, SV:SV + nch], st[:, SL:SL + nch], AF.Exp,
                scale=sb["cv"][:, C_NEGHALF:C_NEGHALF + 1]), inc=s_ac)
            yield
            if which == 1:
                for _ in range(2):
                    yield
            dlist = [None] * nch

            def emit_apply(i):
                kind, k = chunks[i]
                pn, xs, w_src = srcof(kind, k)
                lq = sb["lnq0"] if i % 2 == 0 else sb["lnq1"]
                S.do("vector", lambda e, xs=xs, pn=pn, i=i, lq=lq: e.tensor_scalar(
                    lq[0:pn, :], xs, st[0:pn, SM + i:SM + i + 1],
                    st[0:pn, SV + i:SV + i + 1], ALU.add, ALU.mult),
                    waits=[a2] + w_src + S.need(f"lnq{i % 2}"))
                d4 = S.do("vector", lambda e: e.drain(), inc=s_dv)
                if kind == "al" and which == 1:
                    S.adddep(f"Rraw_cons{par}", d4)
                if kind == "halo":
                    S.setdep("xh_cons", d4)
                dlist[i] = d4

            emit_apply(0)
            if nch > 1:
                emit_apply(1)
            for i, (kind, k) in enumerate(chunks):
                d4 = dlist[i]
                lq = sb["lnq0"] if i % 2 == 0 else sb["lnq1"]
                lqk = f"lnq{i % 2}"
                wpb = S.need("pbf0")
                tr = None
                if kind == "halo":
                    for c in range(8):
                        tr = S.do("tensor", lambda e, c=c, lq=lq: e.transpose(
                            pbf[:, c * 64:(c + 1) * 64],
                            lq[0:64, c * 128:(c + 1) * 128],
                            sb["idb"][0:64, 0:64]),
                            waits=([d4] + wpb + [idb_w]) if c == 0 else [],
                            inc=s_pe if c == 7 else None)
                    S.setdep(lqk, tr)
                    ev = None
                    for c in range(8):
                        ev = S.do("vector", lambda e, c=c, dst=dst, dstw=dstw: e.tensor_copy(
                            dst[:, dstw * c: dstw * c + 32],
                            pbf[:, 64 * c: 64 * c + 32]),
                            waits=[tr] if c == 0 else [])
                        ev = S.do("vector", lambda e, c=c, dst=dst, dstw=dstw: e.tensor_copy(
                            dst[:, dstw * c + 544: dstw * c + 576],
                            pbf[:, 64 * c + 32: 64 * c + 64]),
                            inc=s_dv if c == 7 else None)
                else:
                    for c in range(8):
                        tr = S.do("tensor", lambda e, c=c, lq=lq: e.transpose(
                            pbf[:, c * 128:(c + 1) * 128],
                            lq[:, c * 128:(c + 1) * 128], sb["idb"][:]),
                            waits=([d4] + wpb + [idb_w]) if c == 0 else [],
                            inc=s_pe if c == 7 else None)
                    S.setdep(lqk, tr)
                    lo = (32 + 128 * k) if which == 1 else 128 * k
                    ev = None
                    for c in range(8):
                        ev = S.do("vector", lambda e, c=c, lo=lo, dst=dst, dstw=dstw: e.tensor_copy(
                            dst[:, dstw * c + lo: dstw * c + lo + 128],
                            pbf[:, c * 128: c * 128 + 128]),
                            waits=[tr] if c == 0 else [],
                            inc=s_dv if c == 7 else None)
                S.setdep("pbf0", ev)
                S.adddep(out_key, ev)
                if i + 2 < nch:
                    emit_apply(i + 2)
                yield

        def proj_qk(tile, w):
            if w == "wq":
                dst, dstw, rhs_lo = sb["qat"], 512, 32
            else:
                dst, dstw, rhs_lo = sb["kfm"], 576, 0
            out_key = "qfmw" if w == "wq" else "kfmw"
            lnw = S.need("ln1w")
            rots = [(0, 1), (2, 3)]
            gcount = [0]
            for grp in _ogroups(8, 2):
                n = len(grp)
                gi = GIDX[(w, 0, grp[0] * 128)]
                gb, slot = load_granule(tile, gi)
                wv = gran_loaded[gi]
                banks = rots[gcount[0] % 2][:n]
                gcount[0] += 1
                bw = S.need(*[bkey(b) for b in banks])
                pe_done = None
                for kc in range(8):
                    for j in range(n):
                        oc = grp[j]
                        pe_done = S.do("tensor",
                                       lambda e, kc=kc, j=j, oc=oc, gi=gi, gb=gb, banks=banks, grp=grp, rhs_lo=rhs_lo: e.matmul(
                                           pbank[banks[j]][:, 0:512],
                                           gslice(gi, gb, kc, (oc - grp[0]) * 128, 128),
                                           sb["ln1"][:, 576 * kc + rhs_lo: 576 * kc + rhs_lo + 512],
                                           start=(kc == 0), stop=(kc == 7)),
                                       waits=([wv] + lnw + bw) if (kc == 0 and j == 0) else [],
                                       inc=s_pe if (kc == 7 and j == n - 1) else None)
                for j in range(n):
                    oc = grp[j]
                    ev = S.do("scalar", lambda e, j=j, oc=oc, banks=banks, dst=dst, dstw=dstw: e.copy(
                        dst[:, dstw * oc: dstw * oc + 512], pbank[banks[j]][:, 0:512]),
                        waits=[pe_done], inc=s_ac)
                    S.setdep(bkey(banks[j]), ev)
                    S.adddep(out_key, ev)
                if w == "wk":
                    banks2 = rots[gcount[0] % 2][:n]
                    gcount[0] += 1
                    bw2 = S.need(*[bkey(b) for b in banks2])
                    pe2 = None
                    for kc in range(8):
                        for j in range(n):
                            oc = grp[j]
                            pe2 = S.do("tensor",
                                       lambda e, kc=kc, j=j, oc=oc, gi=gi, gb=gb, banks2=banks2, grp=grp: e.matmul(
                                           pbank[banks2[j]][:, 0:64],
                                           gslice(gi, gb, kc, (oc - grp[0]) * 128, 128),
                                           sb["ln1"][:, 576 * kc + 512: 576 * kc + 576],
                                           start=(kc == 0), stop=(kc == 7)),
                                       waits=bw2 if (kc == 0 and j == 0) else [],
                                       inc=s_pe if (kc == 7 and j == n - 1) else None)
                    for j in range(n):
                        oc = grp[j]
                        ev = S.do("scalar", lambda e, j=j, oc=oc, banks2=banks2: e.copy(
                            sb["kfm"][:, 576 * oc + 512: 576 * oc + 576],
                            pbank[banks2[j]][:, 0:64]),
                            waits=[pe2], inc=s_ac)
                        S.setdep(bkey(banks2[j]), ev)
                        S.adddep(out_key, ev)
                    slot_free[slot].append(pe2)
                else:
                    slot_free[slot].append(pe_done)
                yield

        def proj_tok(tile, w):
            par = tile % 2
            R = Rbuf(tile)
            nts = 5 if w == "wv" else 4
            bank_i = [0]
            for (o0, ocols) in ((0, 384), (384, 384), (768, 256)):
                gi = GIDX[(w, 0, o0)]
                gb, slot = load_granule(tile, gi)
                wv = gran_loaded[gi]
                last = None
                for ts in range(nts):
                    tw = 64 if (w == "wv" and ts == 4) else 128
                    bk = bank_i[0] % 4
                    bank_i[0] += 1
                    bw = S.need(bkey(bk))
                    if w == "wv":
                        srcw = S.need("ln1w")
                    else:
                        srcw = S.need("qfmw2")
                    for kc in range(8):
                        if w == "wv":
                            def fn(e, kc=kc, ts=ts, tw=tw, bk=bk, gi=gi, gb=gb, ocols=ocols):
                                return e.matmul(
                                    pbank[bk][0:tw, 0:ocols],
                                    sb["ln1"][:, 576 * kc + ts * 128: 576 * kc + ts * 128 + tw],
                                    gslice(gi, gb, kc, 0, ocols),
                                    start=(kc == 0), stop=(kc == 7))
                        else:
                            def fn(e, kc=kc, ts=ts, tw=tw, bk=bk, gi=gi, gb=gb, ocols=ocols):
                                return e.matmul(
                                    pbank[bk][0:tw, 0:ocols],
                                    sb["qat"][:, 512 * kc + ts * 128: 512 * kc + ts * 128 + 128],
                                    gslice(gi, gb, kc, 0, ocols),
                                    start=(kc == 0), stop=(kc == 7))
                        last = S.do("tensor", fn,
                                    waits=([wv] + srcw + bw) if kc == 0 else [],
                                    inc=s_pe if kc == 7 else None)
                    if w == "wv":
                        ev = S.do("scalar", lambda e, ts=ts, o0=o0, ocols=ocols, bk=bk, tw=tw: e.copy(
                            sb["vtm"][0:tw, 1024 * ts + o0: 1024 * ts + o0 + ocols],
                            pbank[bk][0:tw, 0:ocols]),
                            waits=[last], inc=s_ac)
                        S.setdep(bkey(bk), ev)
                        S.adddep("vtmw", ev)
                    else:
                        wfs = S.need("fscWO")
                        t1 = S.do("vector", lambda e, bk=bk, o0=o0, ocols=ocols: e.tensor_tensor(
                            sb["fsc"][:, FSC_WO: FSC_WO + ocols], pbank[bk][:, 0:ocols],
                            sb["bo"][:, o0:o0 + ocols], ALU.add),
                            waits=[last] + wfs + consts_done, inc=s_dv)
                        S.setdep(bkey(bk), t1)
                        S.do("vector", lambda e: e.drain())
                        t2 = S.do("vector", lambda e, ts=ts, o0=o0, ocols=ocols, R=R: e.tensor_tensor(
                            R[:, 1024 * ts + o0: 1024 * ts + o0 + ocols],
                            sb["fsc"][:, FSC_WO: FSC_WO + ocols],
                            R[:, 1024 * ts + o0: 1024 * ts + o0 + ocols],
                            ALU.add), waits=S.need(f"Rraw_cons{par}"), inc=s_dv)
                        S.setdep("fscWO", t2)
                        S.adddep(f"Rmid{par}_{ts}", t2)
                    yield
                slot_free[slot].append(last)

        def sched_attention(tile):
            mask_b = sb["msk"].ap().rearrange("p (u c) -> p u c", u=1).broadcast_to([128, 2, 192])
            kw = S.need("kfmw")
            qw = S.need("qfmw")
            vw = S.need("vtmw")
            ascv = sb["fsc"].ap()[:, FSC_A:FSC_A + 384]
            bursts = []
            for a in range(4):
                for hg in range(8):
                    h0 = 4 * (hg // 2) + (hg % 2)
                    bursts.append((a, hg, h0, h0 + 2))
            NB = len(bursts)
            state = [None] * NB

            def part_sc(b):
                a, hg, h0, h1_ = bursts[b]
                sbk = 4 + (b % 2)
                pw = S.need(bkey(sbk))
                sc = None
                for j, hh in enumerate((h0, h1_)):
                    po = 64 * (hh % 2)
                    sc = S.do("tensor", lambda e, j=j, hh=hh, po=po, a=a, sbk=sbk: e.matmul(
                        pbank[sbk][:, 192 * j: 192 * (j + 1)],
                        sb["qat"][po:po + 64,
                                  512 * (hh // 2) + 128 * a: 512 * (hh // 2) + 128 * a + 128],
                        sb["kfm"][po:po + 64,
                                  576 * (hh // 2) + 128 * a: 576 * (hh // 2) + 128 * a + 192],
                        start=True, stop=True),
                        waits=(pw + kw + qw) if j == 0 else [],
                        inc=s_pe if j == 1 else None)
                wasc = S.need("ascw")
                sm = S.do("vector", lambda e, mask_b=mask_b, ascv=ascv, sbk=sbk: e.tensor_tensor(
                    ascv.rearrange("p (u c) -> p u c", c=192),
                    pbank[sbk].ap()[:, 0:384].rearrange("p (u c) -> p u c", c=192),
                    mask_b, ALU.add),
                    waits=[sc] + wasc + consts_done, inc=s_dv)
                S.setdep(bkey(sbk), sm)
                S.do("vector", lambda e: e.drain())
                S.do("vector", lambda e, ascv=ascv: e.reduce_max(
                    sb["sx"][:, 0:1], ascv, axis=AX.X))
                S.do("vector", lambda e: e.drain())
                S.do("vector", lambda e: e.tensor_scalar(
                    sb["sx"][:, 1:2], sb["sx"][:, 0:1],
                    sb["cv"][:, C_NEGSM:C_NEGSM + 1], None, ALU.mult))
                d3 = S.do("vector", lambda e: e.drain(), inc=s_dv)
                act_set("exp")
                a_exp = None
                for u in range(2):
                    a_exp = S.do("scalar", lambda e, u=u, ascv=ascv: e.activation(
                        sb["apb"][:, 192 * u: 192 * (u + 1)],
                        ascv[:, 192 * u: 192 * (u + 1)], AF.Exp,
                        bias=sb["sx"][:, 1:2],
                        scale=sb["cv"][:, C_SMSCALE:C_SMSCALE + 1],
                        accum_out=sb["sx"][:, 2 + u:3 + u]),
                        waits=([d3] + S.need("apbw")) if u == 0 else [],
                        inc=s_ac if u == 1 else None)
                S.setdep("ascw", a_exp)
                S.do("vector", lambda e: e.reciprocal(
                    sb["sx"][:, 4:6], sb["sx"][:, 2:4]), waits=[a_exp])
                S.do("vector", lambda e: e.drain())
                r_b = sb["sx"].ap().rearrange("p (u c) -> p u c", c=1)[:, 4:6, :].broadcast_to([128, 2, 192])
                pr = (b % 2) * 384
                S.do("vector", lambda e, r_b=r_b, pr=pr: e.tensor_tensor(
                    sb["pnb"].ap()[:, pr:pr + 384].rearrange("p (u c) -> p u c", c=192),
                    sb["apb"].ap().rearrange("p (u c) -> p u c", c=192),
                    r_b, ALU.mult), waits=S.need(f"pnbw{b % 2}"))
                d_pn = S.do("vector", lambda e: e.drain(), inc=s_dv)
                S.setdep("apbw", d_pn)
                state[b] = d_pn

            def part_tr(b):
                d_pn = state[b]
                reg = (b % 2) * 512
                pr = (b % 2) * 384
                wpb = S.need("pbf0")
                tr = None
                for j in range(2):
                    tr = S.do("tensor", lambda e, j=j, pr=pr: e.transpose(
                        pbf[:, 256 * j: 256 * j + 128],
                        sb["pnb"][:, pr + 192 * j: pr + 192 * j + 128], sb["idb"][:]),
                        waits=([d_pn] + wpb + [idb_w]) if j == 0 else [])
                    tr = S.do("tensor", lambda e, j=j, pr=pr: e.transpose(
                        pbf[0:64, 256 * j + 128: 256 * j + 256],
                        sb["pnb"][:, pr + 192 * j + 128: pr + 192 * j + 192], sb["idb"][:]),
                        inc=s_pe if j == 1 else None)
                S.setdep(f"pnbw{b % 2}", tr)
                evp = S.do("vector", lambda e, reg=reg: e.tensor_copy(
                    sb["pts"][:, reg: reg + 512], pbf[:, 0:512]),
                    waits=[tr] + S.need(f"ptsw{reg}"), inc=s_dv)
                S.setdep("pbf0", evp)
                state[b] = evp

            def part_pv(b):
                a, hg, h0, h1_ = bursts[b]
                evp = state[b]
                reg = (b % 2) * 512
                pv_last = None
                pw6 = S.need("pb6")
                for j, hh in enumerate((h0, h1_)):
                    pv_last = S.do("tensor",
                                   lambda e, hh=hh, a=a, reg=reg, j=j: e.matmul(
                                       pbank[6][0:64, 128 * j: 128 * j + 128],
                                       sb["vtm"][:, 1024 * a + 64 * hh: 1024 * a + 64 * hh + 64],
                                       sb["pts"][:, reg + 256 * j: reg + 256 * j + 128],
                                       start=True, stop=False),
                                   waits=([evp] + vw + pw6) if j == 0 else [])
                    pv_last = S.do("tensor",
                                   lambda e, hh=hh, a=a, reg=reg, j=j: e.matmul(
                                       pbank[6][0:64, 128 * j: 128 * j + 128],
                                       sb["vtm"][0:64, 1024 * (a + 1) + 64 * hh: 1024 * (a + 1) + 64 * hh + 64],
                                       sb["pts"][0:64, reg + 256 * j + 128: reg + 256 * j + 256],
                                       start=False, stop=True),
                                   inc=s_pe if j == 1 else None)
                S.setdep(f"ptsw{reg}", pv_last)
                ev = None
                for j, hh in enumerate((h0, h1_)):
                    ev = S.do("scalar", lambda e, hh=hh, a=a, j=j: e.copy(
                        sb["qat"][64 * (hh % 2): 64 * (hh % 2) + 64,
                                  512 * (hh // 2) + 128 * a: 512 * (hh // 2) + 128 * a + 128],
                        pbank[6][0:64, 128 * j: 128 * j + 128]),
                        waits=[pv_last] if j == 0 else [], inc=s_ac if j == 1 else None)
                S.setdep("pb6", ev)
                S.adddep("qfmw2", ev)

            for bb in range(0, NB + 4, 2):
                for b in (bb, bb + 1):
                    if 4 <= b < NB + 4:
                        part_pv(b - 4)
                for b in (bb, bb + 1):
                    if 2 <= b < NB + 2:
                        part_tr(b - 2)
                for b in (bb, bb + 1):
                    if b < NB:
                        part_sc(b)
                yield

        def sched_ffn_in(tile):
            ln2w = S.need("ln2w")
            S.dep.pop("ln2w", None)
            for grp in _ogroups(32, 2):
                n = len(grp)
                banksA = (0, 1)[:n]
                banksB = (2, 3)[:n]
                gi1 = GIDX[("w1", 0, grp[0] * 128)]
                gb1, slot1 = load_granule(tile, gi1)
                wv1 = gran_loaded[gi1]
                bwA = S.need(*[bkey(b) for b in banksA])
                lastA = None
                for kc in range(8):
                    for j in range(n):
                        oc = grp[j]
                        lastA = S.do("tensor",
                                     lambda e, kc=kc, j=j, oc=oc, gi1=gi1, gb1=gb1, grp=grp, banksA=banksA: e.matmul(
                                         pbank[banksA[j]][:, 0:512],
                                         gslice(gi1, gb1, kc, (oc - grp[0]) * 128, 128),
                                         sb["ln2"][:, 512 * kc: 512 * (kc + 1)],
                                         start=(kc == 0), stop=(kc == 7)),
                                     waits=([wv1] + ln2w + bwA) if (kc == 0 and j == 0) else [],
                                     inc=s_pe if (kc == 7 and j == n - 1) else None)
                slot_free[slot1].append(lastA)
                act_set("silu")
                wfs = S.need("fsc")
                ev_si = None
                for j in range(n):
                    oc = grp[j]
                    ev_si = S.do("scalar", lambda e, j=j, oc=oc, banksA=banksA: e.activation(
                        sb["fsc"][:, 512 * j: 512 * (j + 1)], pbank[banksA[j]][:, 0:512],
                        AF.Silu, scale=sb["s1"][:, oc:oc + 1]),
                        waits=([lastA] + wfs + consts_done) if j == 0 else [], inc=s_ac)
                for j in range(n):
                    S.setdep(bkey(banksA[j]), ev_si)
                gi2 = GIDX[("w2", 0, grp[0] * 128)]
                gb2, slot2 = load_granule(tile, gi2)
                wv2 = gran_loaded[gi2]
                bwB = S.need(*[bkey(b) for b in banksB])
                lastB = None
                for kc in range(8):
                    for j in range(n):
                        oc = grp[j]
                        lastB = S.do("tensor",
                                     lambda e, kc=kc, j=j, oc=oc, gi2=gi2, gb2=gb2, grp=grp, banksB=banksB: e.matmul(
                                         pbank[banksB[j]][:, 0:512],
                                         gslice(gi2, gb2, kc, (oc - grp[0]) * 128, 128),
                                         sb["ln2"][:, 512 * kc: 512 * (kc + 1)],
                                         start=(kc == 0), stop=(kc == 7)),
                                     waits=([wv2] + bwB) if (kc == 0 and j == 0) else [],
                                     inc=s_pe if (kc == 7 and j == n - 1) else None)
                slot_free[slot2].append(lastB)
                d_g = None
                for j in range(n):
                    oc = grp[j]
                    d_g = S.do("vector",
                               lambda e, j=j, oc=oc, banksB=banksB: e.scalar_tensor_tensor(
                                   sb["g"][:, 512 * oc: 512 * (oc + 1)],
                                   pbank[banksB[j]][:, 0:512],
                                   sb["s2"][:, oc:oc + 1],
                                   sb["fsc"][:, 512 * j: 512 * (j + 1)],
                                   ALU.mult, ALU.mult),
                               waits=([lastB, ev_si] + consts_done) if j == 0 else [],
                               inc=s_dv)
                for j in range(n):
                    S.setdep(bkey(banksB[j]), d_g)
                S.setdep("fsc", d_g)
                S.adddep("gw", d_g)
                yield

        def sched_h(tile, w, src_name, dst_name, scale_name):
            srcw = S.need(src_name + "w")
            S.dep.pop(src_name + "w", None)
            ab = [0]
            for grp in _ogroups(32, 2):
                n = len(grp)
                banks = ((0, 1) if ab[0] % 2 == 0 else (2, 3))[:n]
                ab[0] += 1
                bw = S.need(*[bkey(b) for b in banks])
                last = None
                for kq in range(4):
                    gi = GIDX[(w, kq * 8, grp[0] * 128)]
                    gb, slot = load_granule(tile, gi)
                    wv = gran_loaded[gi]
                    for kc in range(8):
                        kcg = kq * 8 + kc
                        for j in range(n):
                            oc = grp[j]
                            last = S.do("tensor",
                                        lambda e, kc=kc, kcg=kcg, j=j, oc=oc, gi=gi, gb=gb, grp=grp, banks=banks, src_name=src_name: e.matmul(
                                            pbank[banks[j]][:, 0:512],
                                            gslice(gi, gb, kc, (oc - grp[0]) * 128, 128),
                                            sb[src_name][:, 512 * kcg: 512 * (kcg + 1)],
                                            start=(kcg == 0), stop=(kcg == 31)),
                                        waits=([wv] + srcw + bw) if (kc == 0 and j == 0) else [],
                                        inc=s_pe if (kc == 7 and j == n - 1) else None)
                    slot_free[slot].append(last)
                act_set("gelu")
                ev = None
                for j in range(n):
                    oc = grp[j]
                    ev = S.do("scalar",
                              lambda e, j=j, oc=oc, banks=banks, dst_name=dst_name, scale_name=scale_name: e.activation(
                                  sb[dst_name][:, 512 * oc: 512 * (oc + 1)],
                                  pbank[banks[j]][:, 0:512],
                                  AF.Gelu, scale=sb[scale_name][:, oc:oc + 1]),
                              waits=([last] + consts_done) if j == 0 else [], inc=s_ac)
                for j in range(n):
                    S.setdep(bkey(banks[j]), ev)
                S.adddep(dst_name + "w", ev)
                yield

        def sched_w3(tile):
            par = tile % 2
            R = Rbuf(tile)
            hw = S.need("gw")
            S.dep.pop("gw", None)
            bank_i = [0]
            for oq in range(4):
                gis = [GIDX[("w3", kh * 16, oq * 256)] for kh in range(2)]
                gbslots = [load_granule(tile, gi) for gi in gis]
                gbs = [t[0] for t in gbslots]
                slots = [t[1] for t in gbslots]
                wvs = [gran_loaded[gi] for gi in gis]
                for ts in range(4):
                    bk = bank_i[0] % 4
                    bank_i[0] += 1
                    bw = S.need(bkey(bk))
                    last = None
                    for kh in range(2):
                        for kc in range(16):
                            kcg = kh * 16 + kc
                            last = S.do("tensor",
                                        lambda e, ts=ts, kh=kh, kc=kc, kcg=kcg, bk=bk, oq=oq, gis=gis, gbs=gbs: e.matmul(
                                            pbank[bk][:, 0:256],
                                            sb["g"][:, 512 * kcg + 128 * ts: 512 * kcg + 128 * ts + 128],
                                            gslice(gis[kh], gbs[kh], kc, 0, 256),
                                            start=(kcg == 0), stop=(kcg == 31)),
                                        waits=(wvs + hw + bw) if kcg == 0 else [],
                                        inc=s_pe if kcg == 31 else None)
                    wfs = S.need("fscW3")
                    t1 = S.do("vector", lambda e, bk=bk, oq=oq: e.tensor_tensor(
                        sb["fsc"][:, FSC_W3: FSC_W3 + 256], pbank[bk][:, 0:256],
                        sb["w3s"][:, 256 * oq: 256 * (oq + 1)], ALU.mult),
                        waits=[last] + wfs + consts_done, inc=s_dv)
                    S.setdep(bkey(bk), t1)
                    S.do("vector", lambda e: e.drain())
                    t2 = S.do("vector", lambda e, ts=ts, oq=oq, R=R: e.tensor_tensor(
                        R[:, 1024 * ts + 256 * oq: 1024 * ts + 256 * oq + 256],
                        sb["fsc"][:, FSC_W3: FSC_W3 + 256],
                        R[:, 1024 * ts + 256 * oq: 1024 * ts + 256 * oq + 256],
                        ALU.add), inc=s_dv)
                    S.setdep("fscW3", t2)
                    S.adddep(f"Rfin{par}", t2)
                    yield
                sv = (s_pe, s_pe.n)
                for sl in slots:
                    slot_free[sl].append(sv)

        def load_R(tile):
            par = tile % 2
            R = Rbuf(tile)
            seq, s0 = tile // 4, T * (tile % 4)
            base = seq * L + s0
            wfree = S.need(f"Rfree{par}")
            for k in range(4):
                sv = S.do("scalar", lambda e, base=base, R=R, k=k: e.dma_start(
                    R[:, 1024 * k: 1024 * (k + 1)],
                    x_in[base + 128 * k: base + 128 * k + 128, :]),
                    waits=wfree, inc=s_x, inck=16)
                S.setdep(f"Rw{par}_{k}", sv)

        def load_xh(tile):
            seq, s0 = tile // 4, T * (tile % 4)
            base = seq * L
            waits = S.need("xh_cons")
            S.dep.pop("xhw", None)
            if tile % 4 == 0:
                mz = S.do("gpsimd", lambda e: e.memset(sb["xh"][0:32, :], 0.0),
                          waits=waits, inc=s_pl)
                S.adddep("xhw", mz)
            else:
                sv = S.do("scalar", lambda e, base=base, s0=s0: e.dma_start(
                    sb["xh"][0:32, :], x_in[base + s0 - 32: base + s0, :]),
                    waits=waits, inc=s_x, inck=16)
                S.adddep("xhw", sv)
            if tile % 4 == 3:
                mz = S.do("gpsimd", lambda e: e.memset(sb["xh"][32:64, :], 0.0),
                          waits=waits, inc=s_pl)
                S.adddep("xhw", mz)
            else:
                sv = S.do("scalar", lambda e, base=base, s0=s0: e.dma_start(
                    sb["xh"][32:64, :], x_in[base + s0 + 512: base + s0 + 544, :]),
                    waits=waits, inc=s_x, inck=16)
                S.adddep("xhw", sv)

        def store_out(tile):
            par = tile % 2
            R = Rbuf(tile)
            seq, s0 = tile // 4, T * (tile % 4)
            base = seq * L + s0
            wsrc = S.need(f"Rfin{par}")
            S.dep.pop(f"Rfin{par}", None)
            sv = S.do("scalar", lambda e, base=base, R=R: e.dma_start(
                y_out[base: base + 512, :].rearrange("(ts p) d -> p ts d", p=128),
                R.ap().rearrange("p (ts d) -> p ts d", d=1024)),
                waits=wsrc, inc=s_o, inck=16)
            S.setdep(f"Rfree{par}", sv)

        def dbg_snap(tile, name, buf, depkeys):
            if tile not in DBG_TILES:
                return
            key = f"dbg_{name}_t{tile}"
            ap_shape = [128, buf.shape[1]]
            dt = F32
            dst = nc.dram_tensor(key, ap_shape, dt, kind="ExternalOutput").ap()
            dbg_out[key] = dst
            w = S.need(*depkeys)
            S.do("gpsimd", lambda e, dst=dst, buf=buf: e.dma_start(dst, buf[:, :]),
                 waits=w, inc=s_o, inck=16)

        def bg_gen(tile):
            par = tile % 2
            load_R(tile)
            load_xh(tile)
            yield
            for _ in sched_ln(tile, 1):
                yield
            for _ in proj_qk(tile, "wq"):
                yield
            dbg_snap(tile, "q", sb["qat"], ("qfmw",))
            for _ in proj_qk(tile, "wk"):
                yield
            dbg_snap(tile, "k", sb["kfm"], ("kfmw",))
            for _ in proj_tok(tile, "wv"):
                yield
            dbg_snap(tile, "v", sb["vtm"], ("vtmw",))
            dbg_snap(tile, "ln1", sb["ln1"], ("ln1w",))
            S.dep.pop("ln1w", None)
            for _ in sched_attention(tile):
                yield
            dbg_snap(tile, "attn", sb["qat"], ("qfmw2",))
            S.dep.pop("qfmw", None)
            S.dep.pop("kfmw", None)
            S.dep.pop("vtmw", None)
            for _ in proj_tok(tile, "wo"):
                yield
            S.dep.pop("qfmw2", None)
            S.dep.pop(f"Rraw_cons{par}", None)
            for _ in sched_ln(tile, 2):
                yield
            dbg_snap(tile, "ln2", sb["ln2"], ("ln2w",))
            for k in range(4):
                S.dep.pop(f"Rmid{par}_{k}", None)

        def fg_gen(tile):
            for _ in sched_ffn_in(tile):
                yield
            dbg_snap(tile, "g", sb["g"], ("gw",))
            for _ in sched_h(tile, "h1", "g", "h", "sh1"):
                yield
            dbg_snap(tile, "h", sb["h"], ("hw",))
            for _ in sched_h(tile, "h2", "h", "g", "sh2"):
                yield
            for _ in sched_w3(tile):
                yield

        FG_STEPS = 16 + 16 + 16 + 16
        BG_STEPS = 1 + 9 + 4 + 4 + 15 + 18 + 12 + 6
        PACE = 1.25

        for _ in bg_gen(0):
            pass
        for tile in range(NTILE):
            fg = fg_gen(tile)
            bg = bg_gen(tile + 1) if tile + 1 < NTILE else None
            fi = bi = 0
            for _ in fg:
                fi += 1
                if bg is not None:
                    tgt = min(BG_STEPS, int(BG_STEPS * PACE * fi / FG_STEPS))
                    while bi < tgt:
                        try:
                            next(bg)
                            bi += 1
                        except StopIteration:
                            bg = None
                            break
            while bg is not None:
                try:
                    next(bg)
                    bi += 1
                except StopIteration:
                    bg = None
            store_out(tile)

        if TRUNC is None:
            S.do("scalar", lambda e: e.nop(), waits=[(s_o, s_o.n)])
        print("SCHED total ops:", S.seq)

        block = E(nc.Block())

        @block.tensor
        def _(t):
            S.replay("tensor", t)

        @block.vector
        def _(v):
            S.replay("vector", v)

        @block.scalar
        def _(s):
            S.replay("scalar", s)

        @block.gpsimd
        def _(g):
            S.replay("gpsimd", g)

        @block.sync
        def _(sy):
            S.replay("sync", sy)

    return nc


def _host_inputs(inputs):
    g = {k: np.asarray(v) for k, v in inputs.items()}
    asc = np.ascontiguousarray
    base = {
        "wqT": asc(g["L0_attn_Wq"].T), "wkT": asc(g["L0_attn_Wk"].T),
        "wvT": asc(g["L0_attn_Wv"].T), "woT": asc(g["L0_attn_Wo"].T),
        "w1T": asc(g["L0_ffn_w1_w"].T), "w2T": asc(g["L0_ffn_w2_w"].T),
        "h1T": asc(g["L0_ffn_h1_w"].T), "h2T": asc(g["L0_ffn_h2_w"].T),
        "w3T": asc(g["L0_ffn_w3_w"].T),
        "s1": asc(g["L0_ffn_w1_s"].reshape(32, 128).T),
        "s2": asc(g["L0_ffn_w2_s"].reshape(32, 128).T),
        "sh1": asc(g["L0_ffn_h1_s"].reshape(32, 128).T),
        "sh2": asc(g["L0_ffn_h2_s"].reshape(32, 128).T),
        "bo_bc": asc(np.broadcast_to(g["L0_attn_bo"], (128, D))),
        "w3s_bc": asc(np.broadcast_to(g["L0_ffn_w3_s"], (128, D))),
    }
    r = np.arange(128)[:, None]
    c = np.arange(192)[None, :]
    base["mask"] = np.where((c >= r) & (c < r + WINDOW), 0.0, -1e30).astype(np.float32)
    base["ident"] = np.eye(128, dtype=np.float32)
    cv = np.zeros((128, 8), np.float32)
    cv[:, C_NEGINVD] = -1.0 / D
    cv[:, C_INVD] = 1.0 / D
    cv[:, C_EPS] = 1e-5
    cv[:, C_NEGHALF] = -0.5
    cv[:, C_SMSCALE] = 0.125
    cv[:, C_NEGSM] = -0.125
    base["cvec"] = cv
    x = g["L0_attn_x"].reshape(16, L, D)
    in_maps = []
    for core in range(NCORE):
        m = dict(base)
        m["x"] = asc(x[2 * core: 2 * core + 2].reshape(NSEQ * L, D))
        in_maps.append(m)
    return in_maps


def kernel(**inputs):
    in_maps = _host_inputs(inputs)
    nc = build_program()
    res = run_bass_kernel_spmd(nc, in_maps, core_ids=list(range(NCORE)))
    out = np.concatenate(
        [np.asarray(r["y"]).reshape(NSEQ, L, D) for r in res.results], axis=0)
    return out.astype(np.float32)


# version 15
# speedup vs baseline: 1.1817x; 1.0863x over previous
import numpy as np
from contextlib import ExitStack

import concourse.bass as bass
import concourse.mybir as mybir
from concourse.bass_utils import run_bass_kernel_spmd

F32 = mybir.dt.float32
BF16 = mybir.dt.bfloat16
AF = mybir.ActivationFunctionType
ALU = mybir.AluOpType
AX = mybir.AxisListType

TRUNC = None
DBG_TILES = ()
D, H, DHEAD, WINDOW, PADW = 1024, 16, 64, 64, 32
L, T, DHID = 2048, 512, 4096
NSEQ, NTILE, NCORE = 2, 8, 8

C_NEGINVD, C_INVD, C_EPS, C_NEGHALF, C_SMSCALE, C_NEGSM = 0, 1, 2, 3, 4, 5

WMETA = {
    "wq": (D, D), "wk": (D, D), "wv": (D, D), "wo": (D, D),
    "w1": (D, DHID), "w2": (D, DHID),
    "h1": (DHID, DHID), "h2": (DHID, DHID),
    "w3": (DHID, D),
}


def _ogroups(noc, g):
    out, i = [], 0
    while i < noc:
        n = min(g, noc - i)
        out.append(list(range(i, i + n)))
        i += n
    return out


def granule_plan():
    plan = []
    for w in ("wq", "wk"):
        for grp in _ogroups(8, 2):
            plan.append((w, 0, 8, grp[0] * 128, len(grp) * 128))
    for w in ("wv",):
        for (o0, oc) in ((0, 384), (384, 384), (768, 256)):
            plan.append((w, 0, 8, o0, oc))
    for w in ("wo",):
        for (o0, oc) in ((0, 384), (384, 384), (768, 256)):
            plan.append((w, 0, 8, o0, oc))
    for w in ("w1", "w2"):
        for grp in _ogroups(32, 2):
            plan.append((w, 0, 8, grp[0] * 128, len(grp) * 128))
    for w in ("h1", "h2"):
        for grp in _ogroups(32, 2):
            for kq in range(4):
                plan.append((w, kq * 8, 8, grp[0] * 128, len(grp) * 128))
    for oq in range(4):
        for kh in range(2):
            plan.append(("w3", kh * 16, 16, oq * 256, 256))
    return plan


GRANS = granule_plan()
NGRAN = len(GRANS)
GIDX = {(w, kc0, o0): i for i, (w, kc0, nkc, o0, nc_) in enumerate(GRANS)}
BG_WNAMES = ("wq", "wk", "wv", "wo")
SLOT_BASES = [0, 4096, 8192, 12288, 15360]
FG_SLOTS = (0, 1, 2)
BG_SLOTS = (3, 4)
WSL_COLS = 18432

FSC_F = 0
FSC_A = 1024
FSC_W3 = 1408
FSC_WO = 1664
FSC_COLS = 2048


class Sem:
    def __init__(self, handle, owner):
        self.h = handle
        self.owner = owner
        self.n = 0


class Sched:
    ENGINES = ("tensor", "vector", "scalar", "gpsimd", "sync")

    def __init__(self):
        self.ops = {e: [] for e in self.ENGINES}
        self.waited = {e: {} for e in self.ENGINES}
        self.dep = {}
        self.seq = 0

    def do(self, eng, fn, waits=(), inc=None, inck=1):
        import sys as _sys
        self.last_line = _sys._getframe(1).f_lineno
        real = []
        w = self.waited[eng]
        for sv in waits:
            if sv is None:
                continue
            s, v = sv
            if v <= 0 or s.owner == eng:
                continue
            if w.get(id(s), -1) >= v:
                continue
            real.append((s, v))
            w[id(s)] = v
        best = {}
        for (s, v) in real:
            if v > best.get(id(s), (None, -1))[1]:
                best[id(s)] = (s, v)
        real = list(best.values())
        res = None
        if inc is not None:
            inc.n += inck
            res = (inc, inc.n)
        self.ops[eng].append((real, fn, inc, inck, self.seq))
        self.trace = getattr(self, "trace", [])
        self.trace.append((self.seq, eng, self.last_line,
                           [(id(s) % 1000, v) for (s, v) in real],
                           (id(inc) % 1000 if inc else None, inc.n if inc else None)))
        self.seq += 1
        return res

    def need(self, *keys):
        out = []
        for k in keys:
            out.extend(self.dep.get(k, ()))
        return out

    def setdep(self, key, sv):
        if sv is not None:
            self.dep[key] = [sv]

    def adddep(self, key, sv):
        if sv is not None:
            self.dep.setdefault(key, []).append(sv)

    def replay(self, eng_name, engine):
        for (waits, fn, inc, inck, seq) in self.ops[eng_name]:
            if TRUNC is not None and seq >= TRUNC:
                continue
            for (s, v) in waits:
                engine.wait_ge(s.h, v)
            ins = fn(engine)
            if inc is not None:
                ins.then_inc(inc.h, inck)


def build_program():
    nc = bass.Bass("TRN2", target_bir_lowering=False)

    x_in = nc.dram_tensor("x", [NSEQ * L, D], F32, kind="ExternalInput").ap()
    wt = {w: nc.dram_tensor(w + "T", [k, o], F32, kind="ExternalInput").ap()
          for w, (k, o) in WMETA.items()}
    sc_in = {n: nc.dram_tensor(n, [128, 32], F32, kind="ExternalInput").ap()
             for n in ("s1", "s2", "sh1", "sh2")}
    bo_in = nc.dram_tensor("bo_bc", [128, D], F32, kind="ExternalInput").ap()
    w3s_in = nc.dram_tensor("w3s_bc", [128, D], F32, kind="ExternalInput").ap()
    mask_in = nc.dram_tensor("mask", [128, 192], F32, kind="ExternalInput").ap()
    id_in = nc.dram_tensor("ident", [128, 128], F32, kind="ExternalInput").ap()
    cv_in = nc.dram_tensor("cvec", [128, 8], F32, kind="ExternalInput").ap()
    y_out = nc.dram_tensor("y", [NSEQ * L, D], F32, kind="ExternalOutput").ap()
    dbg_out = {}

    scr = [nc.dram_tensor(f"scr{i}", [128, nkc * ncols], BF16, kind="Internal").ap()
           for i, (w, kc0, nkc, o0, ncols) in enumerate(GRANS)]

    ctx = ExitStack()
    E = ctx.enter_context
    with ctx:
        sb = {}
        sb["lnq0"] = E(nc.sbuf_tensor("lnq0", [128, D], BF16))
        sb["lnq1"] = E(nc.sbuf_tensor("lnq1", [128, D], BF16))
        sb["sqd"] = sb["lnq1"]
        sb["ln1"] = E(nc.sbuf_tensor("ln1", [128, 8 * 576], BF16))
        sb["ln2"] = E(nc.sbuf_tensor("ln2", [128, 8 * 512], BF16))
        sb["qat"] = E(nc.sbuf_tensor("qat", [128, 8 * 512], BF16))
        sb["kfm"] = E(nc.sbuf_tensor("kfm", [128, 8 * 576], BF16))
        sb["vtm"] = E(nc.sbuf_tensor("vtm", [128, 5 * 1024], BF16))
        sb["g"] = E(nc.sbuf_tensor("g", [128, 32 * 512], BF16))
        sb["h"] = E(nc.sbuf_tensor("h", [128, 32 * 512], BF16))
        sb["apb"] = E(nc.sbuf_tensor("apb", [128, 384], BF16))
        sb["pnb"] = E(nc.sbuf_tensor("pnb", [128, 768], BF16))
        sb["pts"] = E(nc.sbuf_tensor("pts", [128, 1024], BF16))
        sb["idb"] = E(nc.sbuf_tensor("idb", [128, 128], BF16))
        sb["wsl"] = E(nc.sbuf_tensor("wsl", [128, WSL_COLS], BF16))
        sb["R0"] = E(nc.sbuf_tensor("R0", [128, 4 * 1024], F32))
        sb["R1"] = E(nc.sbuf_tensor("R1", [128, 4 * 1024], F32))
        sb["xh"] = E(nc.sbuf_tensor("xh", [64, 1024], F32))
        sb["fsc"] = E(nc.sbuf_tensor("fsc", [128, FSC_COLS], F32))
        sb["bo"] = E(nc.sbuf_tensor("bo", [128, D], F32))
        sb["w3s"] = E(nc.sbuf_tensor("w3s", [128, D], F32))
        sb["msk"] = E(nc.sbuf_tensor("msk", [128, 192], F32))
        sb["cv"] = E(nc.sbuf_tensor("cv", [128, 8], F32))
        for n in ("s1", "s2", "sh1", "sh2"):
            sb[n] = E(nc.sbuf_tensor("sb_" + n, [128, 32], F32))
        sb["st"] = E(nc.sbuf_tensor("st", [128, 32], F32))
        sb["sx"] = E(nc.sbuf_tensor("sx", [128, 8], F32))

        pbank = [E(nc.psum_tensor(f"pb{i}", [128, 512], F32)) for i in range(7)]
        pbf = E(nc.psum_tensor("pbf0", [128, 1024], BF16))

        s_pe = Sem(E(nc.semaphore("s_pe")), "tensor")
        s_dv = Sem(E(nc.semaphore("s_dv")), "vector")
        s_ac = Sem(E(nc.semaphore("s_ac")), "scalar")
        s_pl = Sem(E(nc.semaphore("s_pl")), "gpsimd")
        s_wg = Sem(E(nc.semaphore("s_wg")), "qg")
        s_ws = Sem(E(nc.semaphore("s_ws")), "qs")
        s_cv = Sem(E(nc.semaphore("s_cv")), "qs")
        s_x = Sem(E(nc.semaphore("s_x")), "qa")
        s_o = Sem(E(nc.semaphore("s_o")), "qa")

        S = Sched()

        def Rbuf(tile):
            return sb["R0"] if tile % 2 == 0 else sb["R1"]

        def ldc(dst, src):
            return S.do("scalar", lambda e, d=dst, s=src: e.dma_start(d, s),
                        inc=s_x, inck=16)
        for n in ("s1", "s2", "sh1", "sh2"):
            ldc(sb[n][:], sc_in[n])
        ldc(sb["bo"][:], bo_in)
        ldc(sb["w3s"][:], w3s_in)
        ldc(sb["msk"][:], mask_in)
        ldc(sb["cv"][:], cv_in)
        idb_w = S.do("gpsimd", lambda e: e.dma_start(sb["idb"][:], id_in),
                     inc=s_wg, inck=16)
        consts_done = [(s_x, s_x.n), idb_w]

        slot_free = [[] for _ in range(len(SLOT_BASES))]
        slot_ctr = {"fg": 0, "bg": 0}
        gran_loaded = [None] * NGRAN
        scr_written = [None] * NGRAN

        def load_granule(tile, gi):
            w, kc0, nkc, o0, ncols = GRANS[gi]
            pool = "bg" if w in BG_WNAMES else "fg"
            slots = BG_SLOTS if pool == "bg" else FG_SLOTS
            slot = slots[slot_ctr[pool] % len(slots)]
            slot_ctr[pool] += 1
            base = SLOT_BASES[slot]
            dst = sb["wsl"][:, base: base + nkc * ncols]
            waits = list(slot_free[slot])
            slot_free[slot] = []
            if tile == 0:
                src = wt[w].rearrange("(kc p) o -> p kc o", p=128)[:, kc0:kc0 + nkc, o0:o0 + ncols]
                sv = S.do("gpsimd", lambda e, d=dst, s=src: e.dma_start(d, s),
                          waits=waits, inc=s_wg, inck=16)
                gran_loaded[gi] = sv
                wv = S.do("sync", lambda e, d=scr[gi], s=dst: e.dma_start(d, s),
                    waits=[sv], inc=s_cv, inck=16)
                scr_written[gi] = wv
                slot_free[slot].append(wv)
            else:
                q = "gpsimd" if pool == "bg" else "sync"
                qsem = s_wg if pool == "bg" else s_ws
                sv = S.do(q, lambda e, d=dst, s=scr[gi]: e.dma_start(d, s),
                          waits=waits + [scr_written[gi]], inc=qsem, inck=16)
                gran_loaded[gi] = sv
            return base, slot

        def gslice(gi, base, kc_local, o_local, width):
            w, kc0, nkc, o0, ncols = GRANS[gi]
            off = base + kc_local * ncols + o_local
            return sb["wsl"][:, off: off + width]

        def bkey(i):
            return f"pb{i}"

        act_state = ["?"]

        def act_set(name):
            want = {"exp": 6, "ln": 6, "silu": 18, "gelu": 10}[name]
            if act_state[0] == want:
                return
            act_state[0] = want
            S.do("scalar", lambda e: e.drain())
            if want == 6:
                def loadset6(e):
                    return e.add_instruction(mybir.InstLoadActFuncSet(
                        name=e.bass.get_next_instruction_name(),
                        act_func_set_id=6, ins=[], outs=[]))
                S.do("scalar", loadset6)
            else:
                f = {18: AF.Silu, 10: AF.Gelu}[want]
                S.do("scalar", lambda e, f=f: e.preload_act_table(f))
            S.do("scalar", lambda e: e.drain())

        SA, SB, SM, SQ, SV, SL = 0, 5, 10, 15, 20, 25

        def sched_ln(tile, which):
            par = tile % 2
            R = Rbuf(tile)
            if which == 1:
                dst, dstw, out_key = sb["ln1"], 576, "ln1w"
                chunks = [("halo", None)] + [("al", k) for k in range(4)]
            else:
                dst, dstw, out_key = sb["ln2"], 512, "ln2w"
                chunks = [("al", k) for k in range(4)]
            st = sb["st"]
            nch = len(chunks)

            def srcof(kind, k):
                if kind == "halo":
                    return 64, sb["xh"][0:64, :], S.need("xhw")
                if which == 1:
                    return 128, R[:, k * 1024: (k + 1) * 1024], S.need(f"Rw{par}_{k}")
                return 128, R[:, k * 1024: (k + 1) * 1024], S.need(f"Rmid{par}_{k}")

            a1 = None
            d1 = None
            for i, (kind, k) in enumerate(chunks):
                pn, xs, w_src = srcof(kind, k)
                d1 = S.do("vector", lambda e, xs=xs, pn=pn, i=i: e.reduce_sum(
                    st[0:pn, SA + i:SA + i + 1], xs, axis=AX.X),
                    waits=w_src, inc=s_dv)
                a1 = S.do("scalar", lambda e, xs=xs, pn=pn, i=i: e.activation(
                    sb["sqd"][0:pn, :], xs, AF.Square,
                    accum_out=st[0:pn, SB + i:SB + i + 1]),
                    waits=w_src + S.need("lnq1"), inc=s_ac)
            yield
            dA = S.do("vector", lambda e: e.drain(), inc=s_dv)
            S.do("vector", lambda e, nch=nch: e.tensor_scalar(
                st[:, SM:SM + nch], st[:, SA:SA + nch],
                sb["cv"][:, C_NEGINVD:C_NEGINVD + 1], None, ALU.mult),
                waits=consts_done)
            S.do("vector", lambda e, nch=nch: e.tensor_scalar(
                st[:, SQ:SQ + nch], st[:, SB:SB + nch],
                sb["cv"][:, C_INVD:C_INVD + 1], None, ALU.mult),
                waits=[a1])
            S.do("vector", lambda e: e.drain())
            S.do("vector", lambda e, nch=nch: e.tensor_tensor(
                st[:, SV:SV + nch], st[:, SM:SM + nch], st[:, SM:SM + nch],
                ALU.mult))
            S.do("vector", lambda e: e.drain())
            S.do("vector", lambda e, nch=nch: e.tensor_tensor(
                st[:, SV:SV + nch], st[:, SQ:SQ + nch], st[:, SV:SV + nch],
                ALU.subtract))
            d3 = S.do("vector", lambda e: e.drain(), inc=s_dv)
            act_set("ln")
            S.do("scalar", lambda e, nch=nch: e.activation(
                st[:, SL:SL + nch], st[:, SV:SV + nch], AF.Ln,
                bias=sb["cv"][:, C_EPS:C_EPS + 1]), waits=[d3])
            a2 = S.do("scalar", lambda e, nch=nch: e.activation(
                st[:, SV:SV + nch], st[:, SL:SL + nch], AF.Exp,
                scale=sb["cv"][:, C_NEGHALF:C_NEGHALF + 1]), inc=s_ac)
            yield
            if which == 1:
                for _ in range(2):
                    yield
            dlist = [None] * nch

            def emit_apply(i):
                kind, k = chunks[i]
                pn, xs, w_src = srcof(kind, k)
                lq = sb["lnq0"] if i % 2 == 0 else sb["lnq1"]
                S.do("vector", lambda e, xs=xs, pn=pn, i=i, lq=lq: e.tensor_scalar(
                    lq[0:pn, :], xs, st[0:pn, SM + i:SM + i + 1],
                    st[0:pn, SV + i:SV + i + 1], ALU.add, ALU.mult),
                    waits=[a2] + w_src + S.need(f"lnq{i % 2}"))
                d4 = S.do("vector", lambda e: e.drain(), inc=s_dv)
                if kind == "al" and which == 1:
                    S.adddep(f"Rraw_cons{par}", d4)
                if kind == "halo":
                    S.setdep("xh_cons", d4)
                dlist[i] = d4

            emit_apply(0)
            if nch > 1:
                emit_apply(1)
            for i, (kind, k) in enumerate(chunks):
                d4 = dlist[i]
                lq = sb["lnq0"] if i % 2 == 0 else sb["lnq1"]
                lqk = f"lnq{i % 2}"
                wpb = S.need("pbf0")
                tr = None
                if kind == "halo":
                    for c in range(8):
                        tr = S.do("tensor", lambda e, c=c, lq=lq: e.transpose(
                            pbf[:, c * 64:(c + 1) * 64],
                            lq[0:64, c * 128:(c + 1) * 128],
                            sb["idb"][0:64, 0:64]),
                            waits=([d4] + wpb + [idb_w]) if c == 0 else [],
                            inc=s_pe if c == 7 else None)
                    S.setdep(lqk, tr)
                    ev = None
                    for c in range(8):
                        ev = S.do("vector", lambda e, c=c, dst=dst, dstw=dstw: e.tensor_copy(
                            dst[:, dstw * c: dstw * c + 32],
                            pbf[:, 64 * c: 64 * c + 32]),
                            waits=[tr] if c == 0 else [])
                        ev = S.do("vector", lambda e, c=c, dst=dst, dstw=dstw: e.tensor_copy(
                            dst[:, dstw * c + 544: dstw * c + 576],
                            pbf[:, 64 * c + 32: 64 * c + 64]),
                            inc=s_dv if c == 7 else None)
                else:
                    for c in range(8):
                        tr = S.do("tensor", lambda e, c=c, lq=lq: e.transpose(
                            pbf[:, c * 128:(c + 1) * 128],
                            lq[:, c * 128:(c + 1) * 128], sb["idb"][:]),
                            waits=([d4] + wpb + [idb_w]) if c == 0 else [],
                            inc=s_pe if c == 7 else None)
                    S.setdep(lqk, tr)
                    lo = (32 + 128 * k) if which == 1 else 128 * k
                    ev = None
                    for c in range(8):
                        ev = S.do("vector", lambda e, c=c, lo=lo, dst=dst, dstw=dstw: e.tensor_copy(
                            dst[:, dstw * c + lo: dstw * c + lo + 128],
                            pbf[:, c * 128: c * 128 + 128]),
                            waits=[tr] if c == 0 else [],
                            inc=s_dv if c == 7 else None)
                S.setdep("pbf0", ev)
                S.adddep(out_key, ev)
                if i + 2 < nch:
                    emit_apply(i + 2)
                yield

        def proj_qk(tile, w):
            if w == "wq":
                dst, dstw, rhs_lo = sb["qat"], 512, 32
            else:
                dst, dstw, rhs_lo = sb["kfm"], 576, 0
            out_key = "qfmw" if w == "wq" else "kfmw"
            lnw = S.need("ln1w")
            rots = [(0, 1), (2, 3)]
            gcount = [0]
            for grp in _ogroups(8, 2):
                n = len(grp)
                gi = GIDX[(w, 0, grp[0] * 128)]
                gb, slot = load_granule(tile, gi)
                wv = gran_loaded[gi]
                banks = rots[gcount[0] % 2][:n]
                gcount[0] += 1
                bw = S.need(*[bkey(b) for b in banks])
                pe_done = None
                for kc in range(8):
                    for j in range(n):
                        oc = grp[j]
                        pe_done = S.do("tensor",
                                       lambda e, kc=kc, j=j, oc=oc, gi=gi, gb=gb, banks=banks, grp=grp, rhs_lo=rhs_lo: e.matmul(
                                           pbank[banks[j]][:, 0:512],
                                           gslice(gi, gb, kc, (oc - grp[0]) * 128, 128),
                                           sb["ln1"][:, 576 * kc + rhs_lo: 576 * kc + rhs_lo + 512],
                                           start=(kc == 0), stop=(kc == 7)),
                                       waits=([wv] + lnw + bw) if (kc == 0 and j == 0) else [],
                                       inc=s_pe if (kc == 7 and j == n - 1) else None)
                for j in range(n):
                    oc = grp[j]
                    ev = S.do("scalar", lambda e, j=j, oc=oc, banks=banks, dst=dst, dstw=dstw: e.copy(
                        dst[:, dstw * oc: dstw * oc + 512], pbank[banks[j]][:, 0:512]),
                        waits=[pe_done], inc=s_ac)
                    S.setdep(bkey(banks[j]), ev)
                    S.adddep(out_key, ev)
                if w == "wk":
                    banks2 = rots[gcount[0] % 2][:n]
                    gcount[0] += 1
                    bw2 = S.need(*[bkey(b) for b in banks2])
                    pe2 = None
                    for kc in range(8):
                        for j in range(n):
                            oc = grp[j]
                            pe2 = S.do("tensor",
                                       lambda e, kc=kc, j=j, oc=oc, gi=gi, gb=gb, banks2=banks2, grp=grp: e.matmul(
                                           pbank[banks2[j]][:, 0:64],
                                           gslice(gi, gb, kc, (oc - grp[0]) * 128, 128),
                                           sb["ln1"][:, 576 * kc + 512: 576 * kc + 576],
                                           start=(kc == 0), stop=(kc == 7)),
                                       waits=bw2 if (kc == 0 and j == 0) else [],
                                       inc=s_pe if (kc == 7 and j == n - 1) else None)
                    for j in range(n):
                        oc = grp[j]
                        ev = S.do("scalar", lambda e, j=j, oc=oc, banks2=banks2: e.copy(
                            sb["kfm"][:, 576 * oc + 512: 576 * oc + 576],
                            pbank[banks2[j]][:, 0:64]),
                            waits=[pe2], inc=s_ac)
                        S.setdep(bkey(banks2[j]), ev)
                        S.adddep(out_key, ev)
                    slot_free[slot].append(pe2)
                else:
                    slot_free[slot].append(pe_done)
                yield

        def proj_tok(tile, w):
            par = tile % 2
            R = Rbuf(tile)
            nts = 5 if w == "wv" else 4
            bank_i = [0]
            for (o0, ocols) in ((0, 384), (384, 384), (768, 256)):
                gi = GIDX[(w, 0, o0)]
                gb, slot = load_granule(tile, gi)
                wv = gran_loaded[gi]
                last = None
                for ts in range(nts):
                    tw = 64 if (w == "wv" and ts == 4) else 128
                    bk = bank_i[0] % 4
                    bank_i[0] += 1
                    bw = S.need(bkey(bk))
                    if w == "wv":
                        srcw = S.need("ln1w")
                    else:
                        srcw = S.need("qfmw2")
                    for kc in range(8):
                        if w == "wv":
                            def fn(e, kc=kc, ts=ts, tw=tw, bk=bk, gi=gi, gb=gb, ocols=ocols):
                                return e.matmul(
                                    pbank[bk][0:tw, 0:ocols],
                                    sb["ln1"][:, 576 * kc + ts * 128: 576 * kc + ts * 128 + tw],
                                    gslice(gi, gb, kc, 0, ocols),
                                    start=(kc == 0), stop=(kc == 7))
                        else:
                            def fn(e, kc=kc, ts=ts, tw=tw, bk=bk, gi=gi, gb=gb, ocols=ocols):
                                return e.matmul(
                                    pbank[bk][0:tw, 0:ocols],
                                    sb["qat"][:, 512 * kc + ts * 128: 512 * kc + ts * 128 + 128],
                                    gslice(gi, gb, kc, 0, ocols),
                                    start=(kc == 0), stop=(kc == 7))
                        last = S.do("tensor", fn,
                                    waits=([wv] + srcw + bw) if kc == 0 else [],
                                    inc=s_pe if kc == 7 else None)
                    if w == "wv":
                        ev = S.do("scalar", lambda e, ts=ts, o0=o0, ocols=ocols, bk=bk, tw=tw: e.copy(
                            sb["vtm"][0:tw, 1024 * ts + o0: 1024 * ts + o0 + ocols],
                            pbank[bk][0:tw, 0:ocols]),
                            waits=[last], inc=s_ac)
                        S.setdep(bkey(bk), ev)
                        S.adddep("vtmw", ev)
                    else:
                        wfs = S.need("fscWO")
                        t1 = S.do("vector", lambda e, bk=bk, o0=o0, ocols=ocols: e.tensor_tensor(
                            sb["fsc"][:, FSC_WO: FSC_WO + ocols], pbank[bk][:, 0:ocols],
                            sb["bo"][:, o0:o0 + ocols], ALU.add),
                            waits=[last] + wfs + consts_done, inc=s_dv)
                        S.setdep(bkey(bk), t1)
                        S.do("vector", lambda e: e.drain())
                        t2 = S.do("vector", lambda e, ts=ts, o0=o0, ocols=ocols, R=R: e.tensor_tensor(
                            R[:, 1024 * ts + o0: 1024 * ts + o0 + ocols],
                            sb["fsc"][:, FSC_WO: FSC_WO + ocols],
                            R[:, 1024 * ts + o0: 1024 * ts + o0 + ocols],
                            ALU.add), waits=S.need(f"Rraw_cons{par}"), inc=s_dv)
                        S.setdep("fscWO", t2)
                        S.adddep(f"Rmid{par}_{ts}", t2)
                    yield
                slot_free[slot].append(last)

        def sched_attention(tile):
            mask_b = sb["msk"].ap().rearrange("p (u c) -> p u c", u=1).broadcast_to([128, 2, 192])
            kw = S.need("kfmw")
            qw = S.need("qfmw")
            vw = S.need("vtmw")
            ascv = sb["fsc"].ap()[:, FSC_A:FSC_A + 384]
            bursts = []
            for a in range(4):
                for hg in range(8):
                    h0 = 4 * (hg // 2) + (hg % 2)
                    bursts.append((a, hg, h0, h0 + 2))
            NB = len(bursts)
            state = [None] * NB

            def part_sc(b):
                a, hg, h0, h1_ = bursts[b]
                sbk = 4 + (b % 2)
                pw = S.need(bkey(sbk))
                sc = None
                for j, hh in enumerate((h0, h1_)):
                    po = 64 * (hh % 2)
                    sc = S.do("tensor", lambda e, j=j, hh=hh, po=po, a=a, sbk=sbk: e.matmul(
                        pbank[sbk][:, 192 * j: 192 * (j + 1)],
                        sb["qat"][po:po + 64,
                                  512 * (hh // 2) + 128 * a: 512 * (hh // 2) + 128 * a + 128],
                        sb["kfm"][po:po + 64,
                                  576 * (hh // 2) + 128 * a: 576 * (hh // 2) + 128 * a + 192],
                        start=True, stop=True),
                        waits=(pw + kw + qw) if j == 0 else [],
                        inc=s_pe if j == 1 else None)
                wasc = S.need("ascw")
                sm = S.do("vector", lambda e, mask_b=mask_b, ascv=ascv, sbk=sbk: e.tensor_tensor(
                    ascv.rearrange("p (u c) -> p u c", c=192),
                    pbank[sbk].ap()[:, 0:384].rearrange("p (u c) -> p u c", c=192),
                    mask_b, ALU.add),
                    waits=[sc] + wasc + consts_done, inc=s_dv)
                S.setdep(bkey(sbk), sm)
                S.do("vector", lambda e: e.drain())
                S.do("vector", lambda e, ascv=ascv: e.reduce_max(
                    sb["sx"][:, 0:1], ascv, axis=AX.X))
                S.do("vector", lambda e: e.drain())
                S.do("vector", lambda e: e.tensor_scalar(
                    sb["sx"][:, 1:2], sb["sx"][:, 0:1],
                    sb["cv"][:, C_NEGSM:C_NEGSM + 1], None, ALU.mult))
                d3 = S.do("vector", lambda e: e.drain(), inc=s_dv)
                act_set("exp")
                a_exp = None
                for u in range(2):
                    a_exp = S.do("scalar", lambda e, u=u, ascv=ascv: e.activation(
                        sb["apb"][:, 192 * u: 192 * (u + 1)],
                        ascv[:, 192 * u: 192 * (u + 1)], AF.Exp,
                        bias=sb["sx"][:, 1:2],
                        scale=sb["cv"][:, C_SMSCALE:C_SMSCALE + 1],
                        accum_out=sb["sx"][:, 2 + u:3 + u]),
                        waits=([d3] + S.need("apbw")) if u == 0 else [],
                        inc=s_ac if u == 1 else None)
                S.setdep("ascw", a_exp)
                S.do("vector", lambda e: e.reciprocal(
                    sb["sx"][:, 4:6], sb["sx"][:, 2:4]), waits=[a_exp])
                S.do("vector", lambda e: e.drain())
                r_b = sb["sx"].ap().rearrange("p (u c) -> p u c", c=1)[:, 4:6, :].broadcast_to([128, 2, 192])
                pr = (b % 2) * 384
                S.do("vector", lambda e, r_b=r_b, pr=pr: e.tensor_tensor(
                    sb["pnb"].ap()[:, pr:pr + 384].rearrange("p (u c) -> p u c", c=192),
                    sb["apb"].ap().rearrange("p (u c) -> p u c", c=192),
                    r_b, ALU.mult), waits=S.need(f"pnbw{b % 2}"))
                d_pn = S.do("vector", lambda e: e.drain(), inc=s_dv)
                S.setdep("apbw", d_pn)
                state[b] = d_pn

            def part_tr(b):
                d_pn = state[b]
                reg = (b % 2) * 512
                pr = (b % 2) * 384
                wpb = S.need("pbf0")
                tr = None
                for j in range(2):
                    tr = S.do("tensor", lambda e, j=j, pr=pr: e.transpose(
                        pbf[:, 256 * j: 256 * j + 128],
                        sb["pnb"][:, pr + 192 * j: pr + 192 * j + 128], sb["idb"][:]),
                        waits=([d_pn] + wpb + [idb_w]) if j == 0 else [])
                    tr = S.do("tensor", lambda e, j=j, pr=pr: e.transpose(
                        pbf[0:64, 256 * j + 128: 256 * j + 256],
                        sb["pnb"][:, pr + 192 * j + 128: pr + 192 * j + 192], sb["idb"][:]),
                        inc=s_pe if j == 1 else None)
                S.setdep(f"pnbw{b % 2}", tr)
                evp = S.do("vector", lambda e, reg=reg: e.tensor_copy(
                    sb["pts"][:, reg: reg + 512], pbf[:, 0:512]),
                    waits=[tr] + S.need(f"ptsw{reg}"), inc=s_dv)
                S.setdep("pbf0", evp)
                state[b] = evp

            def part_pv(b):
                a, hg, h0, h1_ = bursts[b]
                evp = state[b]
                reg = (b % 2) * 512
                pv_last = None
                pw6 = S.need("pb6")
                for j, hh in enumerate((h0, h1_)):
                    pv_last = S.do("tensor",
                                   lambda e, hh=hh, a=a, reg=reg, j=j: e.matmul(
                                       pbank[6][0:64, 128 * j: 128 * j + 128],
                                       sb["vtm"][:, 1024 * a + 64 * hh: 1024 * a + 64 * hh + 64],
                                       sb["pts"][:, reg + 256 * j: reg + 256 * j + 128],
                                       start=True, stop=False),
                                   waits=([evp] + vw + pw6) if j == 0 else [])
                    pv_last = S.do("tensor",
                                   lambda e, hh=hh, a=a, reg=reg, j=j: e.matmul(
                                       pbank[6][0:64, 128 * j: 128 * j + 128],
                                       sb["vtm"][0:64, 1024 * (a + 1) + 64 * hh: 1024 * (a + 1) + 64 * hh + 64],
                                       sb["pts"][0:64, reg + 256 * j + 128: reg + 256 * j + 256],
                                       start=False, stop=True),
                                   inc=s_pe if j == 1 else None)
                S.setdep(f"ptsw{reg}", pv_last)
                ev = None
                for j, hh in enumerate((h0, h1_)):
                    ev = S.do("scalar", lambda e, hh=hh, a=a, j=j: e.copy(
                        sb["qat"][64 * (hh % 2): 64 * (hh % 2) + 64,
                                  512 * (hh // 2) + 128 * a: 512 * (hh // 2) + 128 * a + 128],
                        pbank[6][0:64, 128 * j: 128 * j + 128]),
                        waits=[pv_last] if j == 0 else [], inc=s_ac if j == 1 else None)
                S.setdep("pb6", ev)
                S.adddep("qfmw2", ev)

            for bb in range(0, NB + 4, 2):
                for b in (bb, bb + 1):
                    if 4 <= b < NB + 4:
                        part_pv(b - 4)
                for b in (bb, bb + 1):
                    if 2 <= b < NB + 2:
                        part_tr(b - 2)
                for b in (bb, bb + 1):
                    if b < NB:
                        part_sc(b)
                yield

        def sched_ffn_in(tile):
            ln2w = S.need("ln2w")
            S.dep.pop("ln2w", None)
            for grp in _ogroups(32, 2):
                n = len(grp)
                banksA = (0, 1)[:n]
                banksB = (2, 3)[:n]
                gi1 = GIDX[("w1", 0, grp[0] * 128)]
                gb1, slot1 = load_granule(tile, gi1)
                wv1 = gran_loaded[gi1]
                bwA = S.need(*[bkey(b) for b in banksA])
                lastA = None
                for kc in range(8):
                    for j in range(n):
                        oc = grp[j]
                        lastA = S.do("tensor",
                                     lambda e, kc=kc, j=j, oc=oc, gi1=gi1, gb1=gb1, grp=grp, banksA=banksA: e.matmul(
                                         pbank[banksA[j]][:, 0:512],
                                         gslice(gi1, gb1, kc, (oc - grp[0]) * 128, 128),
                                         sb["ln2"][:, 512 * kc: 512 * (kc + 1)],
                                         start=(kc == 0), stop=(kc == 7)),
                                     waits=([wv1] + ln2w + bwA) if (kc == 0 and j == 0) else [],
                                     inc=s_pe if (kc == 7 and j == n - 1) else None)
                slot_free[slot1].append(lastA)
                act_set("silu")
                wfs = S.need("fsc")
                ev_si = None
                for j in range(n):
                    oc = grp[j]
                    ev_si = S.do("scalar", lambda e, j=j, oc=oc, banksA=banksA: e.activation(
                        sb["fsc"][:, 512 * j: 512 * (j + 1)], pbank[banksA[j]][:, 0:512],
                        AF.Silu, scale=sb["s1"][:, oc:oc + 1]),
                        waits=([lastA] + wfs + consts_done) if j == 0 else [], inc=s_ac)
                for j in range(n):
                    S.setdep(bkey(banksA[j]), ev_si)
                gi2 = GIDX[("w2", 0, grp[0] * 128)]
                gb2, slot2 = load_granule(tile, gi2)
                wv2 = gran_loaded[gi2]
                bwB = S.need(*[bkey(b) for b in banksB])
                lastB = None
                for kc in range(8):
                    for j in range(n):
                        oc = grp[j]
                        lastB = S.do("tensor",
                                     lambda e, kc=kc, j=j, oc=oc, gi2=gi2, gb2=gb2, grp=grp, banksB=banksB: e.matmul(
                                         pbank[banksB[j]][:, 0:512],
                                         gslice(gi2, gb2, kc, (oc - grp[0]) * 128, 128),
                                         sb["ln2"][:, 512 * kc: 512 * (kc + 1)],
                                         start=(kc == 0), stop=(kc == 7)),
                                     waits=([wv2] + bwB) if (kc == 0 and j == 0) else [],
                                     inc=s_pe if (kc == 7 and j == n - 1) else None)
                slot_free[slot2].append(lastB)
                d_g = None
                for j in range(n):
                    oc = grp[j]
                    d_g = S.do("vector",
                               lambda e, j=j, oc=oc, banksB=banksB: e.scalar_tensor_tensor(
                                   sb["g"][:, 512 * oc: 512 * (oc + 1)],
                                   pbank[banksB[j]][:, 0:512],
                                   sb["s2"][:, oc:oc + 1],
                                   sb["fsc"][:, 512 * j: 512 * (j + 1)],
                                   ALU.mult, ALU.mult),
                               waits=([lastB, ev_si] + consts_done) if j == 0 else [],
                               inc=s_dv)
                for j in range(n):
                    S.setdep(bkey(banksB[j]), d_g)
                S.setdep("fsc", d_g)
                S.adddep("gw", d_g)
                yield

        def sched_h(tile, w, src_name, dst_name, scale_name):
            srcw = S.need(src_name + "w")
            S.dep.pop(src_name + "w", None)
            ab = [0]
            for grp in _ogroups(32, 2):
                n = len(grp)
                banks = ((0, 1) if ab[0] % 2 == 0 else (2, 3))[:n]
                ab[0] += 1
                bw = S.need(*[bkey(b) for b in banks])
                last = None
                for kq in range(4):
                    gi = GIDX[(w, kq * 8, grp[0] * 128)]
                    gb, slot = load_granule(tile, gi)
                    wv = gran_loaded[gi]
                    for kc in range(8):
                        kcg = kq * 8 + kc
                        for j in range(n):
                            oc = grp[j]
                            last = S.do("tensor",
                                        lambda e, kc=kc, kcg=kcg, j=j, oc=oc, gi=gi, gb=gb, grp=grp, banks=banks, src_name=src_name: e.matmul(
                                            pbank[banks[j]][:, 0:512],
                                            gslice(gi, gb, kc, (oc - grp[0]) * 128, 128),
                                            sb[src_name][:, 512 * kcg: 512 * (kcg + 1)],
                                            start=(kcg == 0), stop=(kcg == 31)),
                                        waits=([wv] + srcw + bw) if (kc == 0 and j == 0) else [],
                                        inc=s_pe if (kc == 7 and j == n - 1) else None)
                    slot_free[slot].append(last)
                act_set("gelu")
                ev = None
                for j in range(n):
                    oc = grp[j]
                    ev = S.do("scalar",
                              lambda e, j=j, oc=oc, banks=banks, dst_name=dst_name, scale_name=scale_name: e.activation(
                                  sb[dst_name][:, 512 * oc: 512 * (oc + 1)],
                                  pbank[banks[j]][:, 0:512],
                                  AF.Gelu, scale=sb[scale_name][:, oc:oc + 1]),
                              waits=([last] + consts_done) if j == 0 else [], inc=s_ac)
                for j in range(n):
                    S.setdep(bkey(banks[j]), ev)
                S.adddep(dst_name + "w", ev)
                yield

        def sched_w3(tile):
            par = tile % 2
            R = Rbuf(tile)
            hw = S.need("gw")
            S.dep.pop("gw", None)
            bank_i = [0]
            for oq in range(4):
                gis = [GIDX[("w3", kh * 16, oq * 256)] for kh in range(2)]
                gbslots = [load_granule(tile, gi) for gi in gis]
                gbs = [t[0] for t in gbslots]
                slots = [t[1] for t in gbslots]
                wvs = [gran_loaded[gi] for gi in gis]
                for ts in range(4):
                    bk = bank_i[0] % 4
                    bank_i[0] += 1
                    bw = S.need(bkey(bk))
                    last = None
                    for kh in range(2):
                        for kc in range(16):
                            kcg = kh * 16 + kc
                            last = S.do("tensor",
                                        lambda e, ts=ts, kh=kh, kc=kc, kcg=kcg, bk=bk, oq=oq, gis=gis, gbs=gbs: e.matmul(
                                            pbank[bk][:, 0:256],
                                            sb["g"][:, 512 * kcg + 128 * ts: 512 * kcg + 128 * ts + 128],
                                            gslice(gis[kh], gbs[kh], kc, 0, 256),
                                            start=(kcg == 0), stop=(kcg == 31)),
                                        waits=(wvs + hw + bw) if kcg == 0 else [],
                                        inc=s_pe if kcg == 31 else None)
                    wfs = S.need("fscW3")
                    t1 = S.do("vector", lambda e, bk=bk, oq=oq: e.tensor_tensor(
                        sb["fsc"][:, FSC_W3: FSC_W3 + 256], pbank[bk][:, 0:256],
                        sb["w3s"][:, 256 * oq: 256 * (oq + 1)], ALU.mult),
                        waits=[last] + wfs + consts_done, inc=s_dv)
                    S.setdep(bkey(bk), t1)
                    S.do("vector", lambda e: e.drain())
                    t2 = S.do("vector", lambda e, ts=ts, oq=oq, R=R: e.tensor_tensor(
                        R[:, 1024 * ts + 256 * oq: 1024 * ts + 256 * oq + 256],
                        sb["fsc"][:, FSC_W3: FSC_W3 + 256],
                        R[:, 1024 * ts + 256 * oq: 1024 * ts + 256 * oq + 256],
                        ALU.add), inc=s_dv)
                    S.setdep("fscW3", t2)
                    S.adddep(f"Rfin{par}", t2)
                    yield
                sv = (s_pe, s_pe.n)
                for sl in slots:
                    slot_free[sl].append(sv)

        def load_R(tile):
            par = tile % 2
            R = Rbuf(tile)
            seq, s0 = tile // 4, T * (tile % 4)
            base = seq * L + s0
            wfree = S.need(f"Rfree{par}")
            for k in range(4):
                sv = S.do("scalar", lambda e, base=base, R=R, k=k: e.dma_start(
                    R[:, 1024 * k: 1024 * (k + 1)],
                    x_in[base + 128 * k: base + 128 * k + 128, :]),
                    waits=wfree, inc=s_x, inck=16)
                S.setdep(f"Rw{par}_{k}", sv)

        def load_xh(tile):
            seq, s0 = tile // 4, T * (tile % 4)
            base = seq * L
            waits = S.need("xh_cons")
            S.dep.pop("xhw", None)
            if tile % 4 == 0:
                mz = S.do("gpsimd", lambda e: e.memset(sb["xh"][0:32, :], 0.0),
                          waits=waits, inc=s_pl)
                S.adddep("xhw", mz)
            else:
                sv = S.do("scalar", lambda e, base=base, s0=s0: e.dma_start(
                    sb["xh"][0:32, :], x_in[base + s0 - 32: base + s0, :]),
                    waits=waits, inc=s_x, inck=16)
                S.adddep("xhw", sv)
            if tile % 4 == 3:
                mz = S.do("gpsimd", lambda e: e.memset(sb["xh"][32:64, :], 0.0),
                          waits=waits, inc=s_pl)
                S.adddep("xhw", mz)
            else:
                sv = S.do("scalar", lambda e, base=base, s0=s0: e.dma_start(
                    sb["xh"][32:64, :], x_in[base + s0 + 512: base + s0 + 544, :]),
                    waits=waits, inc=s_x, inck=16)
                S.adddep("xhw", sv)

        def store_out(tile):
            par = tile % 2
            R = Rbuf(tile)
            seq, s0 = tile // 4, T * (tile % 4)
            base = seq * L + s0
            wsrc = S.need(f"Rfin{par}")
            S.dep.pop(f"Rfin{par}", None)
            sv = S.do("scalar", lambda e, base=base, R=R: e.dma_start(
                y_out[base: base + 512, :].rearrange("(ts p) d -> p ts d", p=128),
                R.ap().rearrange("p (ts d) -> p ts d", d=1024)),
                waits=wsrc, inc=s_o, inck=16)
            S.setdep(f"Rfree{par}", sv)

        def dbg_snap(tile, name, buf, depkeys):
            if tile not in DBG_TILES:
                return
            key = f"dbg_{name}_t{tile}"
            ap_shape = [128, buf.shape[1]]
            dt = F32
            dst = nc.dram_tensor(key, ap_shape, dt, kind="ExternalOutput").ap()
            dbg_out[key] = dst
            w = S.need(*depkeys)
            S.do("gpsimd", lambda e, dst=dst, buf=buf: e.dma_start(dst, buf[:, :]),
                 waits=w, inc=s_o, inck=16)

        def bg_gen(tile):
            par = tile % 2
            load_R(tile)
            load_xh(tile)
            yield
            for _ in sched_ln(tile, 1):
                yield
            for _ in proj_qk(tile, "wq"):
                yield
            dbg_snap(tile, "q", sb["qat"], ("qfmw",))
            for _ in proj_qk(tile, "wk"):
                yield
            dbg_snap(tile, "k", sb["kfm"], ("kfmw",))
            for _ in proj_tok(tile, "wv"):
                yield
            dbg_snap(tile, "v", sb["vtm"], ("vtmw",))
            dbg_snap(tile, "ln1", sb["ln1"], ("ln1w",))
            S.dep.pop("ln1w", None)
            for _ in sched_attention(tile):
                yield
            dbg_snap(tile, "attn", sb["qat"], ("qfmw2",))
            S.dep.pop("qfmw", None)
            S.dep.pop("kfmw", None)
            S.dep.pop("vtmw", None)
            for _ in proj_tok(tile, "wo"):
                yield
            S.dep.pop("qfmw2", None)
            S.dep.pop(f"Rraw_cons{par}", None)
            for _ in sched_ln(tile, 2):
                yield
            dbg_snap(tile, "ln2", sb["ln2"], ("ln2w",))
            for k in range(4):
                S.dep.pop(f"Rmid{par}_{k}", None)

        def fg_gen(tile):
            for _ in sched_ffn_in(tile):
                yield
            dbg_snap(tile, "g", sb["g"], ("gw",))
            for _ in sched_h(tile, "h1", "g", "h", "sh1"):
                yield
            dbg_snap(tile, "h", sb["h"], ("hw",))
            for _ in sched_h(tile, "h2", "h", "g", "sh2"):
                yield
            for _ in sched_w3(tile):
                yield

        FG_STEPS = 16 + 16 + 16 + 16
        BG_STEPS = 1 + 9 + 4 + 4 + 15 + 18 + 12 + 6
        PACE = 1.08

        for _ in bg_gen(0):
            pass
        for tile in range(NTILE):
            fg = fg_gen(tile)
            bg = bg_gen(tile + 1) if tile + 1 < NTILE else None
            fi = bi = 0
            for _ in fg:
                fi += 1
                if bg is not None:
                    tgt = min(BG_STEPS, int(BG_STEPS * PACE * fi / FG_STEPS))
                    while bi < tgt:
                        try:
                            next(bg)
                            bi += 1
                        except StopIteration:
                            bg = None
                            break
            while bg is not None:
                try:
                    next(bg)
                    bi += 1
                except StopIteration:
                    bg = None
            store_out(tile)

        if TRUNC is None:
            S.do("scalar", lambda e: e.nop(), waits=[(s_o, s_o.n)])
        print("SCHED total ops:", S.seq)

        block = E(nc.Block())

        @block.tensor
        def _(t):
            S.replay("tensor", t)

        @block.vector
        def _(v):
            S.replay("vector", v)

        @block.scalar
        def _(s):
            S.replay("scalar", s)

        @block.gpsimd
        def _(g):
            S.replay("gpsimd", g)

        @block.sync
        def _(sy):
            S.replay("sync", sy)

    return nc


def _host_inputs(inputs):
    g = {k: np.asarray(v) for k, v in inputs.items()}
    asc = np.ascontiguousarray
    base = {
        "wqT": asc(g["L0_attn_Wq"].T), "wkT": asc(g["L0_attn_Wk"].T),
        "wvT": asc(g["L0_attn_Wv"].T), "woT": asc(g["L0_attn_Wo"].T),
        "w1T": asc(g["L0_ffn_w1_w"].T), "w2T": asc(g["L0_ffn_w2_w"].T),
        "h1T": asc(g["L0_ffn_h1_w"].T), "h2T": asc(g["L0_ffn_h2_w"].T),
        "w3T": asc(g["L0_ffn_w3_w"].T),
        "s1": asc(g["L0_ffn_w1_s"].reshape(32, 128).T),
        "s2": asc(g["L0_ffn_w2_s"].reshape(32, 128).T),
        "sh1": asc(g["L0_ffn_h1_s"].reshape(32, 128).T),
        "sh2": asc(g["L0_ffn_h2_s"].reshape(32, 128).T),
        "bo_bc": asc(np.broadcast_to(g["L0_attn_bo"], (128, D))),
        "w3s_bc": asc(np.broadcast_to(g["L0_ffn_w3_s"], (128, D))),
    }
    r = np.arange(128)[:, None]
    c = np.arange(192)[None, :]
    base["mask"] = np.where((c >= r) & (c < r + WINDOW), 0.0, -1e30).astype(np.float32)
    base["ident"] = np.eye(128, dtype=np.float32)
    cv = np.zeros((128, 8), np.float32)
    cv[:, C_NEGINVD] = -1.0 / D
    cv[:, C_INVD] = 1.0 / D
    cv[:, C_EPS] = 1e-5
    cv[:, C_NEGHALF] = -0.5
    cv[:, C_SMSCALE] = 0.125
    cv[:, C_NEGSM] = -0.125
    base["cvec"] = cv
    x = g["L0_attn_x"].reshape(16, L, D)
    in_maps = []
    for core in range(NCORE):
        m = dict(base)
        m["x"] = asc(x[2 * core: 2 * core + 2].reshape(NSEQ * L, D))
        in_maps.append(m)
    return in_maps


def kernel(**inputs):
    in_maps = _host_inputs(inputs)
    nc = build_program()
    res = run_bass_kernel_spmd(nc, in_maps, core_ids=list(range(NCORE)))
    out = np.concatenate(
        [np.asarray(r["y"]).reshape(NSEQ, L, D) for r in res.results], axis=0)
    return out.astype(np.float32)
